# version 10
# speedup vs baseline: 1.2112x; 1.0079x over previous
import os
import numpy as np
import ml_dtypes
from contextlib import ExitStack
import concourse.bass as bass
import concourse.mybir as mybir

F32 = mybir.dt.float32
BF16 = mybir.dt.bfloat16
AF = mybir.ActivationFunctionType
ALU = mybir.AluOpType
AX = mybir.AxisListType.X

T, HALF, D = 4096, 2048, 1024
NPT = 8
NTC = 4
WIN = 512
HID = 2816
NHH = 22
PI = float(np.pi)
ENGINES = ("sync", "vector", "scalar", "tensor", "gpsimd")


class Op:
    __slots__ = ("fn", "inc")
    def __init__(self, fn):
        self.fn = fn
        self.inc = None


class Kern:
    def __init__(self, taps=(), stop_after=None):
        self.taps = set(taps)
        self.stop_after = stop_after
        self.nc = bass.Bass("TRN2", target_bir_lowering=False, debug=False, detect_race_conditions=(os.environ.get("RACED", "0") == "1"))
        self.q = {e: [] for e in ENGINES}
        self.sems = {}
        self.es = ExitStack()
        self.dram = {}
        self.tap_tensors = {}
        self._dirty = {e: None for e in ENGINES}
        self.engsem = {"sync": "sx", "vector": "sv", "scalar": "ss",
                       "tensor": "st", "gpsimd": "sw"}

    def sem(self, name):
        h = self.es.enter_context(self.nc.semaphore(name))
        self.sems[name] = [h, 0]

    def E(self, eng, method, *args, inc=None, **kw):
        op = Op(None)
        sems = self.sems
        def fn(e, method=method, args=args, kw=kw, op=op):
            inst = getattr(e, method)(*args, **kw)
            if op.inc is not None:
                inst.then_inc(sems[op.inc[0]][0], op.inc[1])
        op.fn = fn
        self.q[eng].append(op)
        if inc is not None:
            op.inc = inc
            sems[inc[0]][1] += inc[1]
            self._dirty[eng] = None
            return sems[inc[0]][1]
        self._dirty[eng] = op
        return None

    def W(self, eng, semname, val):
        if val <= 0:
            return
        h = self.sems[semname][0]
        self.q[eng].append(Op(lambda e, h=h, v=val: e.wait_ge(h, v)))

    def cnt(self, name):
        return self.sems[name][1]

    def close(self, eng, n=1):
        op = self._dirty[eng]
        nm = self.engsem[eng]
        if op is not None and op.inc is None:
            op.inc = (nm, n)
            self.sems[nm][1] += n
            self._dirty[eng] = None
        return self.sems[nm][1]

    def bar(self, engines=ENGINES):
        for e in ENGINES:
            self.close(e)
        if not hasattr(self, "_lastbar"):
            self._lastbar = {}
        vals = []
        for n in self.sems:
            if n == "swc":
                continue
            v = self.cnt(n)
            if self._lastbar.get(n) != v:
                vals.append((n, v))
                if engines is ENGINES:
                    self._lastbar[n] = v
        for e in engines:
            for n, v in vals:
                self.W(e, n, v)

    def declare_dram(self):
        nc, dr = self.nc, self.dram
        def di(name, shape, dt=F32):
            dr[name] = nc.dram_tensor(name, shape, dt, kind="ExternalInput").ap()
        di("x0t", [D, T])
        di("wqk", [D, 64]); di("wv", [D, 1024]); di("wop", [1024, D])
        for l in range(2):
            di(f"wqkv{l}", [D, 1536]); di(f"wo{l}", [512, D])
            di(f"w1p{l}", [D, 2 * HID]); di(f"w2p{l}", [HID, D])
            di(f"b1p{l}", [128, 2 * NHH]); di(f"b2c{l}", [128, NPT])
            di(f"lng{l}", [128, NPT]); di(f"lnb{l}", [128, NPT])
            di(f"qs{l}", [128, 1]); di(f"ks{l}", [128, 1])
        for n, sh in (("onesf", [1, 128]), ("epscol", [128, 1]),
                      ("eps6col", [128, 1]), ("pihalf", [128, 1])):
            di(n, sh, F32)
        for n, sh in (("maskl", [128, 128]), ("maskr", [128, 128]),
                      ("onesbb", [128, 128]),
                      ("rotp", [128, 128]), ("selck", [16, 8 * 128]),
                      ("segsel", [128, 8 * 16]), ("identb", [128, 128]),
                      ("onesb", [1, 128]), ("onescb", [128, 1])):
            di(n, sh, BF16)
        for n in ("tcs", "tss", "tci", "tsi"):
            di(n, [128, T], BF16)
        dr["out"] = nc.dram_tensor("out", [D, T], F32, kind="ExternalOutput").ap()
        dr["x1s"] = nc.dram_tensor("x1s", [D, T], F32, kind="Internal").ap()
        for l in range(2):
            dr[f"w1pb{l}"] = nc.dram_tensor(f"w1pb{l}", [D, 2 * HID], BF16,
                                            kind="Internal").ap()
            dr[f"w2pb{l}"] = nc.dram_tensor(f"w2pb{l}", [HID, D], BF16,
                                            kind="Internal").ap()
            dr[f"wqkvb{l}"] = nc.dram_tensor(f"wqkvb{l}", [D, 1536], BF16,
                                             kind="Internal").ap()
            dr[f"wob{l}"] = nc.dram_tensor(f"wob{l}", [512, D], BF16,
                                           kind="Internal").ap()

    def tap(self, name, shape, dt=F32):
        if name in self.tap_tensors:
            return self.tap_tensors[name]
        ap = self.nc.dram_tensor("tap_" + name, shape, dt,
                                 kind="ExternalOutput").ap()
        self.tap_tensors[name] = ap
        return ap

    def alloc(self):
        nc, es = self.nc, self.es
        s = {}
        hdl = {}
        def t(name, shape, dt):
            hdl[name] = es.enter_context(nc.sbuf_tensor("s_" + name, shape, dt))
            s[name] = hdl[name][:]
        t("x", [128, NPT * (HALF + 1)], F32)
        t("xb", [128, NPT * (HALF + 1)], BF16)
        t("qbuf", [128, 4 * WIN], BF16)
        for l in range(2):
            t(f"kbuf{l}", [128, 4 * 2 * WIN], BF16)
            t(f"vbuf{l}", [128, 2 * 4 * 8 * 65], BF16)
        t("AR1", [128, 12288], BF16)
        t("AR2", [128, 8192], BF16)
        t("AR3", [128, 4096], BF16)
        t("AR4", [128, 4096], BF16)
        t("AR5", [128, 2048], BF16)
        t("SA", [128, 2048], BF16)
        t("SB", [128, 512], F32)
        t("ROWS", [1, 1024], F32)
        for n, sh, dt in (
            ("onesf", [1, 128], F32), ("epscol", [128, 1], F32),
            ("eps6col", [128, 1], F32), ("pihalf", [128, 1], F32),
            ("maskl", [128, 128], BF16), ("maskr", [128, 128], BF16),
            ("onesbb", [128, 128], BF16),
            ("rotp", [128, 128], BF16), ("selck", [16, 8 * 128], BF16),
            ("segsel", [128, 8 * 16], BF16), ("identb", [128, 128], BF16),
            ("onesb", [1, 128], BF16), ("onescb", [128, 1], BF16),
        ):
            t(n, sh, dt)
        for l in range(2):
            t(f"b1p{l}", [128, 2 * NHH], F32); t(f"b2c{l}", [128, NPT], F32)
            t(f"lng{l}", [128, NPT], F32); t(f"lnb{l}", [128, NPT], F32)
            t(f"qs{l}", [128, 1], F32); t(f"ks{l}", [128, 1], F32)
        t("gxr", [128, 2 * NPT], F32)
        t("gxb", [128, 2 * NPT], BF16)
        t("negm4", [128, 4], BF16)
        t("negmT", [4, 128], BF16)
        t("carry", [128, 24], F32)
        ar1, ar2, ar3 = s["AR1"], s["AR2"], s["AR3"]
        ar4, ar5, sa, sb = s["AR4"], s["AR5"], s["SA"], s["SB"]
        ar1f = ar1.bitcast(F32); ar2f = ar2.bitcast(F32)
        ar3f = ar3.bitcast(F32)
        s["wqkv"] = ar1[:, 0:12288]
        s["heads"] = ar1[:, 0:6144]
        s["wos"] = ar1[:, 6144:10240]
        s["tsk"] = ar1[0:32, 10240:12288]
        s["cumKc"] = ar1f[0:32, 0:2048]
        s["cumKs"] = ar1f[0:32, 2048:4096]
        s["den"] = s["cumKc"]
        s["tcq"] = ar4[0:32, 0:2048]
        s["tsq"] = ar4[0:32, 2048:4096]
        s["cqr"] = s["tcq"]
        s["sqr"] = s["tsq"]
        s["tck"] = s["qbuf"][0:32, 0:2048]
        s["tcs"] = ar2[:, 0:2048]; s["tss"] = ar2[:, 2048:4096]
        s["tci"] = ar2[:, 4096:6144]; s["tsi"] = ar2[:, 6144:8192]
        s["vfm"] = ar2f[:, 0:2048]
        s["cums"] = s["vfm"]
        s["wbld"] = ar2f[:, 2048:4096]
        s["wo"] = ar3[:, 0:4096]
        s["cumc"] = ar3f[:, 0:2048]
        s["w1c"] = ar4[:, 0:4096]
        s["obuf"] = ar4[:, 0:2048]
        s["w2c"] = ar5[:, 0:2048]
        s["wvs"] = ar5[:, 0:2048]
        s["wqk"] = ar5[:, 0:512]
        s["trigt"] = sb[0:64, 0:512]
        s["strow"] = s["ROWS"][0:1, 0:1024]
        s["negrow"] = s["ROWS"].bitcast(BF16)[0:1, 0:512]
        s["rdlrow"] = s["ROWS"][0:1, 512:1024]
        s["onesc"] = sa[:, 0:2048]
        self.s = s
        al = {}
        al["sq2"] = sa[:, 1536:2048]
        al["xc"] = sa[:, 1024:1536]
        al["gelu"] = sb[:, 0:512]
        al["t1"] = sb[:, 0:512]
        al["x1buf"] = ar3f[:, 0:2048]
        al["ssc"] = sb[0:16, 0:512]
        al["rr"] = sa[0:16, 1024:1536]
        al["prepA"] = sa[:, 0:512]
        al["prepB"] = sa[:, 512:1024]
        al["qkraw"] = ar4[:, 0:4096]
        al["ubuf"] = sa[:, 0:1280]
        al["oraw"] = sa[:, 1280:1792]
        al["msb"] = ar1[:, 0:11264]
        self.a = al

    def psums(self):
        nc, es = self.nc, self.es
        p = {}
        for n in ("pa", "pb", "pc", "pd"):
            p[n] = es.enter_context(nc.psum_tensor(n, [128, 512], F32))
        p["pst0"] = es.enter_context(nc.psum_tensor("pst0", [128, 640], F32))
        p["pst1"] = es.enter_context(nc.psum_tensor("pst1", [128, 640], F32))
        p["ptp"] = p["pd"][0:4, 0:64].bitcast(BF16)
        self.p = p

    def xcol(self, kk, tk0, n, shifted=False):
        off = kk * (HALF + 1) + 1 + tk0 - (1 if shifted else 0)
        return self.s["x"][:, off:off + n]

    def xbcol(self, kk, tk0, n, shifted=False):
        off = kk * (HALF + 1) + 1 + tk0 - (1 if shifted else 0)
        return self.s["xb"][:, off:off + n]

    def kap(self, l, f, slot, t0, n):
        return self.s[f"kbuf{l}"][:, f * 1024 + slot * 512 + t0:
                                  f * 1024 + slot * 512 + t0 + n]

    def vap(self, l, slot, tt, h):
        base = slot * (4 * 8 * 65) + tt * (8 * 65) + h * 65
        return self.s[f"vbuf{l}"][:, base:base + 65]

    def build(self):
        for n in ("sx", "sw", "sv", "ss", "st", "smd", "sst", "sda", "sdb",
                  "sdc", "sdd", "sfa", "sfb", "sfc", "sfd", "sme", "swc"):
            self.sem(n)
        self.declare_dram()
        self.alloc()
        self.psums()
        self.emit_const_loads()
        if self.stop_after != "consts":
            self.emit_wconv()
        self.bar()
        done = False
        if self.stop_after == "consts":
            self.emit_taps(); self.finish()
            blk = self.es.enter_context(self.nc.Block())
            for eng in ENGINES:
                ops = self.q[eng]
                def runner(e, ops=ops):
                    for op in ops:
                        op.fn(e)
                getattr(blk, eng)(runner)
            self.es.close()
            return self.nc
        for half in (0, 1):
            self.half = half
            self.emit_x0_load(half)
            if self.stop_after == "x0":
                self.bar()
                break
            self.emit_outer(half)
            if self.stop_after in ("o1", "o2", "o3", "oln"):
                break
            self.bar()
            self.mtap(f"x_outer{half}", self.s["x"],
                      [128, NPT * (HALF + 1)])
            if self.stop_after == f"outer{half}":
                done = True; break
            for l in (0, 1):
                self.emit_layer(half, l)
                self.mtap(f"x_l{half}_{l}", self.s["x"],
                          [128, NPT * (HALF + 1)])
                self.mtap(f"xb_l{half}_{l}", self.s["xb"],
                          [128, NPT * (HALF + 1)], BF16)
                if self.stop_after == f"layer{half}_{l}":
                    done = True; break
            if done:
                break
            self.emit_final(half)
            self.bar()
            self.mtap(f"xfin{half}", self.s["x"], [128, NPT * (HALF + 1)], F32)
            if self.stop_after == f"final{half}":
                done = True; break
        self.emit_taps()
        self.finish()
        blk = self.es.enter_context(self.nc.Block())
        for eng in ENGINES:
            ops = self.q[eng]
            def runner(e, ops=ops):
                for op in ops:
                    op.fn(e)
            getattr(blk, eng)(runner)
        self.es.close()
        return self.nc

    def emit_const_loads(self):
        s, dr = self.s, self.dram
        names = ["onesf", "epscol", "eps6col", "pihalf", "maskl", "maskr",
                 "onesbb",
                 "rotp", "selck", "segsel", "identb", "onesb", "onescb"]
        for l in range(2):
            names += [f"b1p{l}", f"b2c{l}", f"lng{l}", f"lnb{l}", f"qs{l}",
                      f"ks{l}"]
        for n in names:
            self.E("sync", "dma_start", out=s[n], in_=dr[n], inc=("sx", 16))
        v = self.cnt("sx")
        self.W("vector", "sx", v)
        for l in range(2):
            vb = s[f"vbuf{l}"]
            self.E("vector", "memset",
                   vb.rearrange("p (g e) -> p g e", e=65)[:, :, 64:65], 1.0)
        self.E("vector", "memset", s["gxr"], 0.0)
        self.E("vector", "memset", s["gxb"], 0.0)
        self.E("vector", "memset", s["x"], 0.0)
        self.E("vector", "memset", s["xb"], 0.0)
        self.E("vector", "memset", s["carry"], 0.0)

    def emit_wconv(self):
        dr = self.dram
        for l in range(2):
            for (a, b_, rows) in ((f"w1p{l}", f"w1pb{l}", D),
                                  (f"w2p{l}", f"w2pb{l}", HID),
                                  (f"wqkv{l}", f"wqkvb{l}", D),
                                  (f"wo{l}", f"wob{l}", 512)):
                step = rows // 4
                for i in range(4):
                    self.E("gpsimd", "dma_start",
                           out=dr[b_][i * step:(i + 1) * step, :],
                           in_=dr[a][i * step:(i + 1) * step, :],
                           inc=("swc", 16))
        self.wconv_done = self.cnt("swc")

    def emit_x0_load(self, half):
        s, dr = self.s, self.dram
        self.E("gpsimd", "dma_start",
               out=s["wqk"].rearrange("p (a n) -> p a n", a=8),
               in_=dr["wqk"].rearrange("(a p) n -> p a n", p=128),
               inc=("sw", 16))
        self.x0wqk = self.cnt("sw")
        PS = ("sfa", "sfb", "sfc", "sfd")
        self.x0ps = PS
        self.x0g = {}
        for tc in range(NTC):
            for kk in range(NPT):
                self.E("sync", "dma_start",
                       out=self.xcol(kk, tc * WIN, WIN),
                       in_=dr["x0t"][kk * 128:(kk + 1) * 128,
                                     half * HALF + tc * WIN:
                                     half * HALF + (tc + 1) * WIN],
                       inc=(PS[tc], 16))
            self.x0g[tc] = self.cnt(PS[tc])

    def emit_outer(self, half):
        s, p, dr, a_ = self.s, self.p, self.dram, self.a
        self.E("vector", "memset", s["onesc"], 1.0)
        tanh_done = {}
        self.W("tensor", "sw", self.x0wqk)
        for tc in range(NTC):
            pv_ = p["pa"] if tc % 2 == 0 else p["pb"]
            self.W("vector", self.x0ps[tc], self.x0g[tc])
            for kk in range(NPT):
                self.E("vector", "tensor_copy",
                       out=self.xbcol(kk, tc * WIN, WIN),
                       in_=self.xcol(kk, tc * WIN, WIN))
            cast_done = self.close("vector")
            self.W("tensor", "sv", cast_done)
            if tc >= 2:
                self.W("tensor", "ss", tanh_done[tc - 2])
            for kk in range(NPT):
                self.E("tensor", "matmul", out=pv_[0:64, :],
                       lhsT=s["wqk"][:, kk * 64:(kk + 1) * 64],
                       rhs=self.xbcol(kk, tc * WIN, WIN),
                       start=(kk == 0), stop=(kk == NPT - 1))
            tv = self.close("tensor")
            self.W("scalar", "st", tv)
            tanh_done[tc] = self.E("scalar", "activation", out=s["trigt"],
                   in_=pv_[0:64, :], func=AF.Tanh, inc=("ss", 1))
            c0 = tc * WIN
            self.E("scalar", "activation", out=s["tcq"][:, c0:c0 + WIN],
                   in_=s["trigt"][0:32, :], func=AF.Sin,
                   bias=s["pihalf"][0:32, :], scale=PI / 4)
            self.E("scalar", "activation", out=s["tsq"][:, c0:c0 + WIN],
                   in_=s["trigt"][0:32, :], func=AF.Sin, scale=PI / 4)
            self.E("scalar", "activation", out=s["tck"][:, c0:c0 + WIN],
                   in_=s["trigt"][32:64, :], func=AF.Sin,
                   bias=s["pihalf"][0:32, :], scale=PI / 4)
            self.E("scalar", "activation", out=s["tsk"][:, c0:c0 + WIN],
                   in_=s["trigt"][32:64, :], func=AF.Sin, scale=PI / 4,
                   inc=("ss", 1))
        if self.stop_after == "o1":
            return
        sv = self.close("scalar")
        self.W("vector", "ss", sv)
        ca = s["carry"]
        for (src, dst, cc) in ((s["tck"], s["cumKc"], 16),
                               (s["tsk"], s["cumKs"], 17)):
            init = ca[0:32, cc:cc + 1] if half == 1 else 0.0
            vv = self.E("vector", "tensor_tensor_scan", out=dst[:],
                        data0=s["onesc"][0:32, :], data1=src[:], initial=init,
                        op0=ALU.mult, op1=ALU.add, inc=("sv", 1))
            self.W("vector", "sv", vv)
            self.E("vector", "tensor_copy", out=ca[0:32, cc:cc + 1],
                   in_=dst[:, HALF - 1:HALF])
        self.E("vector", "tensor_mul", out=s["den"], in0=s["tcq"],
               in1=s["cumKc"])
        self.E("vector", "tensor_mul", out=s["cumKs"], in0=s["tsq"],
               in1=s["cumKs"])
        vv = self.E("vector", "tensor_add", out=s["den"], in0=s["den"],
                    in1=s["cumKs"], inc=("sv", 1))
        self.W("scalar", "sv", vv)
        self.E("scalar", "activation", out=s["den"], in_=s["den"],
               func=AF.Ln, bias=s["eps6col"][0:32, :])
        sv = self.E("scalar", "activation", out=s["den"], in_=s["den"],
                    func=AF.Exp, scale=-1.0, inc=("ss", 1))
        self.W("vector", "ss", sv)
        self.E("vector", "tensor_mul", out=s["cqr"], in0=s["tcq"],
               in1=s["den"])
        self.E("vector", "tensor_mul", out=s["sqr"], in0=s["tsq"],
               in1=s["den"])
        self.bar()
        self.mtap(f"caro{half}", self.s["carry"], [128, 24], F32)
        self.mtap("cumk", self.s["AR1"][0:32, 0:8192], [32, 8192], BF16)
        self.mtap("tcq", self.s["AR4"][0:32, 0:2048], [32, 2048], BF16)
        self.mtap("tsq", self.s["AR4"][0:32, 2048:4096], [32, 2048], BF16)
        self.mtap("tck", self.s["qbuf"][0:32, 0:2048], [32, 2048], BF16)
        self.mtap("tsk", self.s["AR1"][0:32, 10240:12288], [32, 2048], BF16)
        self.mtap("rd", self.s["AR1"][0:32, 0:4096], [32, 4096], BF16)
        if self.stop_after == "o2":
            return
        self.vproj_t = {}
        for (g0, gn) in ((0, 3), (3, 3), (6, 2)):
            self.W("gpsimd", "st", self.cnt("st"))
            wos_done = []
            for ai in range(gn):
                a = g0 + ai
                self.E("gpsimd", "dma_start",
                       out=s["wos"][:, ai * 1024:(ai + 1) * 1024],
                       in_=dr["wop"][a * 128:(a + 1) * 128, :], inc=("sw", 16))
                wos_done.append(self.cnt("sw"))
            for ai in range(gn):
                a = g0 + ai
                wb = ai % 2
                wv_done = self.E("gpsimd", "dma_start",
                       out=s["wvs"][:, wb * 1024:(wb + 1) * 1024]
                           .rearrange("p (c n) -> p c n", c=8),
                       in_=dr["wv"][:, a * 128:(a + 1) * 128]
                           .rearrange("(c p) n -> p c n", p=128),
                       inc=("sw", 16))
                evacs = {}
                self.W("tensor", "sv", self.cnt("sv"))
                for tc in range(NTC):
                    pv = p["pa"] if tc % 2 == 0 else p["pb"]
                    self.W("tensor", "sw", wv_done)
                    if tc >= 2:
                        self.W("tensor", "sv", evacs[tc - 2])
                    for kk in range(NPT):
                        self.E("tensor", "matmul", out=pv[:],
                               lhsT=s["wvs"][:, wb * 1024 + kk * 128:
                                             wb * 1024 + (kk + 1) * 128],
                               rhs=self.xbcol(kk, tc * WIN, WIN),
                               start=(kk == 0), stop=(kk == NPT - 1))
                    tv = self.close("tensor")
                    self.vproj_t[a] = tv
                    self.W("vector", "st", tv)
                    evacs[tc] = self.E("vector", "tensor_copy",
                                       out=s["vfm"][:, tc * WIN:(tc + 1) * WIN],
                                       in_=pv[:], inc=("sv", 1))
                if os.environ.get("O3LIM") == "vproj":
                    continue
                self.bar()
                for (tt_, cum, cc) in ((s["tck"], s["cumc"], 0),
                                       (s["tsk"], s["cums"], 8)):
                    self.W("tensor", "sv", self.cnt("sv"))
                    for tc in range(NTC):
                        pcb = p["pc"] if tc % 2 == 0 else p["pd"]
                        if tc >= 2:
                            self.W("tensor", "sv", self.cnt("sv"))
                        if os.environ.get("CKMODE") == "id":
                            self.E("tensor", "matmul", out=pcb[:],
                                   lhsT=s["identb"],
                                   rhs=s["onesc"][:, 0:512],
                                   start=True, stop=True)
                        else:
                            self.E("tensor", "matmul", out=pcb[:],
                                   lhsT=s["selck"][:, a * 128:(a + 1) * 128],
                                   rhs=tt_[0:16, tc * WIN:(tc + 1) * WIN],
                                   start=True, stop=True)
                        tv = self.close("tensor")
                        self.W("vector", "st", tv)
                        wmode = os.environ.get("WMODE", "")
                        if wmode == "copy":
                            self.E("vector", "tensor_copy",
                                   out=s["wbld"][:, tc * WIN:(tc + 1) * WIN],
                                   in_=pcb[:], inc=("sv", 1))
                        elif wmode == "novfm":
                            self.E("vector", "tensor_mul",
                                   out=s["wbld"][:, tc * WIN:(tc + 1) * WIN],
                                   in0=pcb[:], in1=pcb[:], inc=("sv", 1))
                        else:
                            self.E("vector", "tensor_mul",
                                   out=s["wbld"][:, tc * WIN:(tc + 1) * WIN],
                                   in0=pcb[:],
                                   in1=s["vfm"][:, tc * WIN:(tc + 1) * WIN],
                                   inc=("sv", 1))
                    init = ca[:, cc + a:cc + a + 1] if half == 1 else 0.0
                    vv = self.E("vector", "tensor_tensor_scan", out=cum[:],
                                data0=s["onesc"], data1=s["wbld"],
                                initial=init, op0=ALU.mult, op1=ALU.add,
                                inc=("sv", 1))
                    self.W("vector", "sv", vv)
                    self.E("vector", "tensor_copy",
                           out=ca[:, cc + a:cc + a + 1],
                           in_=cum[:, HALF - 1:HALF], inc=("sv", 1))
                if os.environ.get("O3LIM") == "wbuild":
                    continue
                self.bar()
                scans_done = self.cnt("sv")
                for tc in range(NTC):
                    pq = p["pc"] if tc % 2 == 0 else p["pd"]
                    ps_ = p["pa"] if tc % 2 == 0 else p["pb"]
                    self.W("tensor", "sv", scans_done)
                    self.E("tensor", "matmul", out=pq[:],
                           lhsT=s["selck"][:, a * 128:(a + 1) * 128],
                           rhs=s["cqr"][0:16, tc * WIN:(tc + 1) * WIN],
                           start=True, stop=True)
                    self.E("tensor", "matmul", out=ps_[:],
                           lhsT=s["selck"][:, a * 128:(a + 1) * 128],
                           rhs=s["sqr"][0:16, tc * WIN:(tc + 1) * WIN],
                           start=True, stop=True)
                    tv = self.close("tensor")
                    self.W("vector", "st", tv)
                    h0 = ai * HALF + tc * WIN
                    self.E("vector", "tensor_mul", out=s["wbld"][:, 0:WIN],
                           in0=pq[:], in1=s["cumc"][:, tc * WIN:(tc + 1) * WIN])
                    self.E("vector", "tensor_mul", out=s["wbld"][:, WIN:2 * WIN],
                           in0=ps_[:], in1=s["cums"][:, tc * WIN:(tc + 1) * WIN])
                    self.E("vector", "tensor_add",
                           out=s["heads"][:, h0:h0 + WIN],
                           in0=s["wbld"][:, 0:WIN], in1=s["wbld"][:, WIN:2 * WIN],
                           inc=("sv", 1))
                    self.W("tensor", "sv", self.cnt("sv"))
                self.bar()
                if a == int(os.environ.get("TAPA", "-1")):
                    self.mtap("headsA",
                              self.s["heads"][:, ai * HALF:(ai + 1) * HALF],
                              [128, HALF], BF16)
            if g0 == 0:
                self.bar()
                self.mtap("heads0", self.s["AR1"][:, 0:6144], [128, 6144], BF16)
            if os.environ.get("O3LIM") in ("vproj", "wbuild", "heads"):
                self.bar()
                continue
            heads_done = self.close("vector")
            self.W("tensor", "sv", heads_done)
            for ai in range(gn):
                self.W("tensor", "sw", wos_done[ai])
            evacs = {}
            for tc in range(NTC):
                for dt_ in range(NPT):
                    i = tc * NPT + dt_
                    pd_ = p["pa"] if i % 2 == 0 else p["pb"]
                    if i >= 2:
                        self.W("tensor", "sv", evacs[i - 2])
                    for ai in range(gn):
                        self.E("tensor", "matmul", out=pd_[:],
                               lhsT=s["wos"][:, ai * 1024 + dt_ * 128:
                                             ai * 1024 + (dt_ + 1) * 128],
                               rhs=s["heads"][:, ai * HALF + tc * WIN:
                                              ai * HALF + (tc + 1) * WIN],
                               start=(ai == 0), stop=(ai == gn - 1))
                    tv = self.close("tensor")
                    self.W("vector", "st", tv)
                    evacs[i] = self.E("vector", "tensor_add",
                                      out=self.xcol(dt_, tc * WIN, WIN),
                                      in0=pd_[:],
                                      in1=self.xcol(dt_, tc * WIN, WIN),
                                      inc=("sv", 1))
            self.bar()
        self.mtap("xpre", self.s["x"], [128, NPT * (HALF + 1)], F32)
        if self.stop_after == "o3":
            return
        self.emit_ln(l=None, shifted=False)
        for kk in range(NPT):
            self.E("sync", "dma_start",
                   out=dr["x1s"][kk * 128:(kk + 1) * 128,
                                 half * HALF:(half + 1) * HALF],
                   in_=self.xcol(kk, 0, HALF), inc=("sx", 16))
        if self.half == 0:
            self.save_gxb(0)
        v = self.cnt("st")
        self.ffn_prev1 = {0: v, 1: v}
        self.ffn_prev2 = {0: v, 1: v}

    def emit_ln(self, l, shifted, write_xb=True):
        s, p, a_ = self.s, self.p, self.a
        self.bar()
        xcb = [a_["xc"], a_["prepA"]]
        sqb = [a_["sq2"], a_["prepB"]]
        norm_done = {-2: 0, -1: 0}
        chain_ss = {}
        self.stmm = {}

        def pair(tc):
            return (p["pc"], p["pd"]) if tc % 2 == 0 else (p["pa"], p["pb"])

        def stats(tc):
            sp, sq = pair(tc)
            if tc >= 2:
                self.W("tensor", "sv", norm_done[tc - 2])
            for kk in range(NPT):
                i = tc * NPT + kk
                b = i % 2
                sh = shifted and kk >= 4
                xa = self.xcol(kk, tc * WIN, WIN, shifted=sh)
                if i >= 2:
                    self.W("vector", "st", self.stmm[i - 2])
                    self.W("scalar", "st", self.stmm[i - 2])
                if kk % 2 == 0:
                    self.E("vector", "tensor_copy", out=xcb[b], in_=xa)
                    vv = self.close("vector")
                    sv = self.E("scalar", "activation", out=sqb[b], in_=xa,
                                func=AF.Square, inc=("ss", 1))
                else:
                    sv = self.E("scalar", "activation", out=xcb[b], in_=xa,
                                func=AF.Identity, inc=("ss", 1))
                    self.E("vector", "tensor_mul", out=sqb[b], in0=xa, in1=xa)
                    vv = self.close("vector")
                self.W("tensor", "sv", vv)
                self.W("tensor", "ss", sv)
                self.E("tensor", "matmul", out=sp[:], lhsT=s["onesbb"],
                       rhs=xcb[b], start=(kk == 0), stop=(kk == NPT - 1))
                tv = self.E("tensor", "matmul", out=sq[:], lhsT=s["onesbb"],
                            rhs=sqb[b], start=(kk == 0), stop=(kk == NPT - 1),
                            inc=("st", 1))
                self.stmm[i] = tv

        def chain(tc):
            sp, sq = pair(tc)
            self.W("vector", "st", self.stmm[tc * NPT + NPT - 1])
            vv = self.E("vector", "tensor_scalar", out=sp[:], in0=sp[:],
                        scalar1=1.0 / 1024.0, scalar2=None, op0=ALU.mult,
                        inc=("sv", 1))
            self.W("scalar", "sv", vv)
            sv = self.E("scalar", "activation", out=a_["xc"], in_=sp[:],
                        func=AF.Square, inc=("ss", 1))
            self.E("vector", "tensor_scalar", out=sq[:], in0=sq[:],
                   scalar1=1.0 / 1024.0, scalar2=None, op0=ALU.mult)
            self.W("vector", "ss", sv)
            vv = self.E("vector", "tensor_sub", out=sq[:], in0=sq[:],
                        in1=a_["xc"], inc=("sv", 1))
            self.W("scalar", "sv", vv)
            self.E("scalar", "activation", out=sq[:], in_=sq[:], func=AF.Ln,
                   bias=s["epscol"])
            sv = self.E("scalar", "activation", out=sq[:], in_=sq[:],
                        func=AF.Exp, scale=-0.5, inc=("ss", 1))
            chain_ss[tc] = sv

        def norm(tc):
            sp, sq = pair(tc)
            self.W("vector", "ss", chain_ss[tc])
            for kk in range(NPT):
                sh = shifted and kk >= 4
                xa = self.xcol(kk, tc * WIN, WIN, shifted=sh)
                if l is not None:
                    self.W("vector", "ss", self.cnt("ss"))
                self.E("vector", "tensor_sub", out=a_["t1"], in0=xa,
                       in1=sp[:])
                if l is None:
                    if write_xb:
                        vv = self.E("vector", "tensor_mul", out=xa,
                                    in0=a_["t1"], in1=sq[:], inc=("sv", 1))
                        self.W("scalar", "sv", vv)
                        self.E("scalar", "activation",
                               out=self.xbcol(kk, tc * WIN, WIN), in_=xa,
                               func=AF.Identity, inc=("ss", 1))
                    else:
                        self.E("vector", "tensor_mul", out=xa, in0=a_["t1"],
                               in1=sq[:])
                else:
                    vv = self.E("vector", "tensor_mul", out=a_["t1"],
                                in0=a_["t1"], in1=sq[:], inc=("sv", 1))
                    self.W("scalar", "sv", vv)
                    self.E("scalar", "activation",
                           out=self.xbcol(kk, tc * WIN, WIN), in_=a_["t1"],
                           func=AF.Identity, scale=s[f"lng{l}"][:, kk:kk + 1],
                           bias=s[f"lnb{l}"][:, kk:kk + 1], inc=("ss", 1))
            norm_done[tc] = self.close("vector")
            self.close("scalar")

        for tc in range(NTC):
            stats(tc)
            chain(tc)
            if tc >= 1:
                norm(tc - 1)
        norm(NTC - 1)
        self.bar()

    def save_gxb(self, l):
        s = self.s
        self.E("vector", "tensor_copy", out=s["gxb"][:, l * NPT:(l + 1) * NPT],
               in_=s["xb"].rearrange("p (k c) -> p k c", c=HALF + 1)
                   [:, :, HALF:HALF + 1].rearrange("p k c -> p (k c)"))

    def save_gxr(self, l):
        s = self.s
        self.E("vector", "tensor_copy", out=s["gxr"][:, l * NPT:(l + 1) * NPT],
               in_=s["x"].rearrange("p (k c) -> p k c", c=HALF + 1)
                   [:, :, HALF:HALF + 1].rearrange("p k c -> p (k c)"))

    def load_guards(self, l, which):
        s = self.s
        if which == "x":
            self.E("vector", "tensor_copy",
                   out=s["x"].rearrange("p (k c) -> p k c", c=HALF + 1)
                       [:, :, 0:1].rearrange("p k c -> p (k c)"),
                   in_=s["gxr"][:, l * NPT:(l + 1) * NPT])
        else:
            self.E("vector", "tensor_copy",
                   out=s["xb"].rearrange("p (k c) -> p k c", c=HALF + 1)
                       [:, :, 0:1].rearrange("p k c -> p (k c)"),
                   in_=s["gxb"][:, l * NPT:(l + 1) * NPT])

    def emit_layer(self, half, l):
        s, p, dr, a_ = self.s, self.p, self.dram, self.a
        if l == 1:
            for kk in range(NPT):
                self.E("vector", "tensor_copy", out=self.xbcol(kk, 0, HALF),
                       in_=self.xcol(kk, 0, HALF))
            if half == 0:
                self.save_gxb(1)
        if half == 1:
            self.load_guards(l, "xb")
        NW = os.environ.get("NOWL", "")
        if l == 0 and "t" not in NW:
            for n in ("tcs", "tss", "tci", "tsi"):
                self.E("sync", "dma_start", out=s[n],
                       in_=dr[n][:, half * HALF:(half + 1) * HALF],
                       inc=("sx", 16))
        self.W("gpsimd", "swc", self.wconv_done)
        if "q" not in NW:
            self.E("gpsimd", "dma_start",
                   out=s["wqkv"].rearrange("p (c n) -> p c n", c=8),
                   in_=dr[f"wqkvb{l}"].rearrange("(c p) n -> p c n", p=128),
                   inc=("sw", 16))
        if "o" not in NW:
            self.E("gpsimd", "dma_start",
                   out=s["wo"].rearrange("p (c n) -> p c n", c=4),
                   in_=dr[f"wob{l}"].rearrange("(c p) n -> p c n", p=128),
                   inc=("sw", 16))
        self.bar()
        LL = os.environ.get("LL", "qkv,attn,wo,ln,ffn").split(",")
        for tc in range(NTC):
            self.emit_qkv(half, l, tc)
            self.bar()
            if "attn" not in LL:
                continue
            self.emit_attn(half, l, tc)
            self.bar()
            if "wo" not in LL:
                continue
            self.emit_wo(half, l, tc)
            self.bar()
        self.mtap(f"xwo{half}_{l}", self.s["x"], [128, NPT * (HALF + 1)], F32)
        if "ln" not in LL:
            return
        if half == 0:
            self.save_gxr(l)
        if half == 1:
            self.load_guards(l, "x")
        self.W("gpsimd", "swc", self.wconv_done)
        self.W("gpsimd", "st", self.cnt("st"))
        self.ffn_pre = {}
        for hh in (0, 1):
            psem = "sda" if hh % 2 == 0 else "sdb"
            self.ffn_pre[hh] = self.E("gpsimd", "dma_start",
                   out=s["w1c"][:, hh * 2048:(hh + 1) * 2048]
                       .rearrange("p (c n) -> p c n", c=8),
                   in_=dr[f"w1pb{l}"][:, hh * 256:(hh + 1) * 256]
                       .rearrange("(c p) n -> p c n", p=128),
                   inc=(psem, 16))
        self.emit_ln(l=l, shifted=True)
        self.mtap(f"hn{half}_{l}", self.s["xb"], [128, NPT * (HALF + 1)], BF16)
        if "ffn" not in LL:
            return
        self.emit_ffn(half, l)
        self.bar()

    def emit_qkv(self, half, l, tc):
        s, p, a_ = self.s, self.p, self.a
        slot = (half * NTC + tc) % 2
        c0 = tc * WIN
        qsc = s[f"qs{l}"]; ksc = s[f"ks{l}"]
        evac = {}
        vevac = {}
        for f in range(8):
            pf = p["pa"] if f % 2 == 0 else p["pb"]
            if f >= 2:
                self.W("tensor", "ss", evac[f - 2])
                self.W("tensor", "sv", vevac[f - 2])
            col0 = f * 128 if f < 4 else 512 + (f - 4) * 128
            _sh = os.environ.get("NOSHIFT", "") != "1"
            for kk in range(NPT):
                self.E("tensor", "matmul", out=pf[:],
                       lhsT=s["wqkv"][:, kk * 1536 + col0:kk * 1536 + col0 + 128],
                       rhs=self.xbcol(kk, c0, WIN, shifted=(_sh and kk >= 4)),
                       start=(kk == 0), stop=(kk == NPT - 1))
            tv = self.close("tensor")
            self.W("vector", "st", tv)
            self.W("scalar", "st", tv)
            vevac[f] = self.E("vector", "tensor_copy",
                   out=a_["qkraw"][:, f * 512:(f + 1) * 512], in_=pf[:],
                   inc=("sv", 1))
            self.W("scalar", "sv", vevac[f])
            sv = self.E("scalar", "activation", out=a_["sq2"], in_=pf[:],
                        func=AF.Square, inc=("ss", 1))
            evac[f] = sv
            self.W("tensor", "ss", sv)
            self.E("tensor", "matmul", out=p["pd"][0:16, :],
                   lhsT=s["segsel"][:, f * 16:(f + 1) * 16], rhs=a_["sq2"],
                   start=(f == 0), stop=(f == 7))
            tv = self.close("tensor")
            self.W("scalar", "st", tv)
        segdone = self.cnt("st")
        if os.environ.get("QL") == "qk":
            return
        qkevac = self.cnt("sv")
        vev = {}
        for tt in range(4):
            pf = p["pa"] if tt % 2 == 0 else p["pb"]
            if tt < 2:
                self.W("tensor", "sv", qkevac)
                self.W("tensor", "ss", self.cnt("ss"))
            else:
                self.W("tensor", "sv", vev[tt - 2])
            for kk in range(NPT):
                self.E("tensor", "matmul", out=pf[:],
                       lhsT=self.xbcol(kk, c0 + tt * 128, 128, shifted=(kk >= 4)),
                       rhs=s["wqkv"][:, kk * 1536 + 1024:kk * 1536 + 1536],
                       start=(kk == 0), stop=(kk == NPT - 1))
            tv = self.close("tensor")
            self.W("vector", "st", tv)
            base = slot * (4 * 8 * 65) + tt * (8 * 65)
            out = s[f"vbuf{l}"][:, base:base + 520] \
                .rearrange("p (h e) -> p h e", e=65)[:, :, 0:64]
            vev[tt] = self.E("vector", "tensor_copy", out=out,
                             in_=pf[:].rearrange("p (h e) -> p h e", e=64),
                             inc=("sv", 1))
        if os.environ.get("QL") == "v":
            return
        self.W("vector", "st", segdone)
        vv = self.E("vector", "tensor_scalar", out=a_["ssc"],
                    in0=p["pd"][0:16, :], scalar1=1e-24, scalar2=None,
                    op0=ALU.max, inc=("sv", 1))
        self.W("scalar", "sv", vv)
        self.E("scalar", "activation", out=a_["ssc"], in_=a_["ssc"], func=AF.Ln)
        rrdone = self.E("scalar", "activation", out=a_["rr"], in_=a_["ssc"],
                        func=AF.Exp, scale=-0.5, inc=("ss", 1))
        if os.environ.get("QL") == "stats":
            return
        self.W("tensor", "ss", rrdone)
        self.W("tensor", "sv", self.cnt("sv"))
        for f in range(8):
            prb = p["pc"] if f % 2 == 0 else p["pd"]
            prot = p["pa"] if f % 2 == 0 else p["pb"]
            self.E("tensor", "matmul", out=prb[:],
                   lhsT=s["selck"][:, f * 128:(f + 1) * 128], rhs=a_["rr"],
                   start=True, stop=True)
            tv = self.close("tensor")
            self.W("vector", "st", tv)
            vv = self.E("vector", "scalar_tensor_tensor", out=a_["prepA"],
                        in0=a_["qkraw"][:, f * 512:(f + 1) * 512],
                        scalar=(qsc if f < 4 else ksc)[:],
                        in1=prb[:], op0=ALU.mult, op1=ALU.mult, inc=("sv", 1))
            self.W("tensor", "sv", vv)
            self.E("tensor", "matmul", out=prot[:], lhsT=s["rotp"],
                   rhs=a_["prepA"], start=True, stop=True)
            tv = self.close("tensor")
            self.W("vector", "st", tv)
            tab1 = s["tcs"] if f < 4 else s["tci"]
            tab2 = s["tss"] if f < 4 else s["tsi"]
            self.E("vector", "tensor_mul", out=a_["prepB"], in0=a_["prepA"],
                   in1=tab1[:, c0:c0 + WIN])
            self.E("vector", "tensor_mul", out=a_["prepA"], in0=prot[:],
                   in1=tab2[:, c0:c0 + WIN])
            if f < 4:
                dst = s["qbuf"][:, f * 512:(f + 1) * 512]
            else:
                dst = self.kap(l, f - 4, slot, 0, 512)
            vv = self.E("vector", "tensor_add", out=dst, in0=a_["prepB"],
                        in1=a_["prepA"], inc=("sv", 1))
            self.W("tensor", "sv", vv)

    def emit_attn(self, half, l, tc):
        s, p, a_ = self.s, self.p, self.a
        wg = half * NTC + tc
        first = (wg == 0)
        slot_cur = wg % 2
        slot_prev = 1 - slot_cur
        NR = [s["ROWS"].bitcast(BF16)[0:1, 0:512],
              s["SB"].bitcast(BF16)[0:1, 0:512]]
        SEM = ["smd", "sme"]
        red = {}
        dmad = {}
        obufv = {-1: 0}

        def A_mm(h, it):
            f = h // 2
            pb_ = (h % 2) * 64
            jlo = 512 if first else it * 128
            jhi = it * 128 + 640
            cuts = sorted({jlo, jhi, 512, it * 128 + 512})
            cuts = [c for c in cuts if jlo <= c <= jhi]
            pieces = [(cuts[i], cuts[i + 1]) for i in range(len(cuts) - 1)
                      if cuts[i + 1] > cuts[i]]
            if it >= 1:
                self.W("tensor", "sv", red[h][it - 1])
            bankA = [q for q in pieces if q[0] - it * 128 < 512]
            bankB = [q for q in pieces if q[0] - it * 128 >= 512]
            for q in pieces:
                a0, b0 = q
                sl = slot_prev if a0 < 512 else slot_cur
                t0 = a0 - 512 if a0 >= 512 else a0
                c0_, c1_ = a0 - it * 128, b0 - it * 128
                grp = bankA if c0_ < 512 else bankB
                self.E("tensor", "matmul", out=p["pst0"][:, c0_:c1_],
                       lhsT=s["qbuf"][pb_:pb_ + 64,
                                      f * 512 + it * 128:f * 512 + it * 128 + 128],
                       rhs=self.kap(l, f, sl, t0, b0 - a0)[pb_:pb_ + 64, :],
                       start=(q == grp[0]), stop=(q == grp[-1]))
            tv = self.close("tensor")
            self.W("vector", "st", tv)
            red[h][it] = self.E("vector", "reduce_max",
                                out=s["negm4"][:, it:it + 1],
                                in_=p["pst0"][:, jlo - it * 128:640],
                                axis=AX, negate=True, inc=("sv", 1))

        def B(h):
            self.W("tensor", "sv", red[h][3])
            tv = self.E("tensor", "transpose", out=p["ptp"], in_=s["negm4"],
                        identity=s["identb"], inc=("st", 1))
            self.W("vector", "st", tv)
            vv = self.E("vector", "tensor_copy", out=s["negmT"], in_=p["ptp"],
                        inc=("sv", 1))
            self.W("scalar", "sv", vv)
            dmad[h] = self.E("scalar", "dma_start", out=NR[h % 2],
                             in_=s["negmT"], inc=(SEM[h % 2], 16))

        def C(h, it):
            f = h // 2
            pb_ = (h % 2) * 64
            ss0 = (4 - it) if first else 0
            for sidx in range(ss0, 5):
                g = it + sidx
                sl = slot_prev if g < 4 else slot_cur
                tt_in = g % 4
                self.E("tensor", "matmul",
                       out=p["pst1"][:, sidx * 128:(sidx + 1) * 128],
                       lhsT=self.kap(l, f, sl, tt_in * 128, 128)[pb_:pb_ + 64, :],
                       rhs=s["qbuf"][pb_:pb_ + 64,
                                     f * 512 + it * 128:f * 512 + it * 128 + 128],
                       start=(sidx == ss0 or sidx == 4), stop=False)
            self.W("tensor", SEM[h % 2], dmad[h])
            for sidx in range(ss0, 5):
                stopA = (ss0 > 0 and sidx == 3)
                self.E("tensor", "matmul",
                       out=p["pst1"][:, sidx * 128:(sidx + 1) * 128],
                       lhsT=s["onesb"],
                       rhs=NR[h % 2][0:1, it * 128:(it + 1) * 128],
                       start=False, stop=stopA)
            if ss0 == 0:
                self.E("tensor", "matmul", out=p["pst1"][:, 0:128],
                       lhsT=s["identb"], rhs=s["maskl"],
                       start=False, stop=True)
            tv = self.E("tensor", "matmul", out=p["pst1"][:, 512:640],
                        lhsT=s["identb"], rhs=s["maskr"],
                        start=False, stop=True, inc=("st", 1))
            self.W("scalar", "st", tv)
            n = (5 - ss0) * 128
            ub = a_["ubuf"][:, (it % 2) * 640:(it % 2) * 640 + n]
            sv = self.E("scalar", "activation", out=ub,
                        in_=p["pst1"][:, ss0 * 128:640], func=AF.Exp,
                        inc=("ss", 1))
            return ss0, ub, sv

        def PV(h, it, ss0, ub, sv):
            self.W("tensor", "ss", sv)
            for sidx in range(ss0, 5):
                g = it + sidx
                sl = slot_prev if g < 4 else slot_cur
                tt_in = g % 4
                self.E("tensor", "matmul",
                       out=p["pa"][0:65, it * 128:(it + 1) * 128],
                       lhsT=self.vap(l, sl, tt_in, h),
                       rhs=ub[:, (sidx - ss0) * 128:(sidx - ss0 + 1) * 128],
                       start=(it == 0 and sidx == ss0),
                       stop=(it == 3 and sidx == 4))

        def E_(h):
            f = h // 2
            pb_ = (h % 2) * 64
            pvdone = self.close("tensor")
            self.W("scalar", "st", pvdone)
            self.E("scalar", "activation", out=s["rdlrow"],
                   in_=p["pa"][64:65, :], func=AF.Ln)
            self.E("scalar", "activation", out=s["rdlrow"],
                   in_=s["rdlrow"], func=AF.Exp, scale=-1.0)
            sv = self.E("scalar", "copy", out=a_["oraw"][0:64, :],
                        in_=p["pa"][0:64, :], inc=("ss", 1))
            self.W("tensor", "sv", obufv[h - 1])
            self.W("tensor", "ss", sv)
            tv = self.E("tensor", "matmul", out=p["pc"][0:64, :],
                        lhsT=s["onesf"][:, 0:64], rhs=s["rdlrow"],
                        start=True, stop=True, inc=("st", 1))
            self.W("vector", "st", tv)
            obufv[h] = self.E("vector", "tensor_mul",
                              out=s["obuf"][pb_:pb_ + 64, f * 512:(f + 1) * 512],
                              in0=a_["oraw"][0:64, :], in1=p["pc"][0:64, :],
                              inc=("sv", 1))

        for h in range(9):
            if h < 8:
                red[h] = {}
                A_mm(h, 0)
            for it in range(4):
                cc = C(h - 1, it) if h >= 1 else None
                if h < 8 and it < 3:
                    A_mm(h, it + 1)
                if h >= 1:
                    PV(h - 1, it, *cc)
            if h < 8:
                B(h)
            if h >= 1:
                E_(h - 1)
        self.W("tensor", "sv", obufv[7])

    def emit_wo(self, half, l, tc):
        s, p = self.s, self.p
        c0 = tc * WIN
        evac = {}
        for dt_ in range(NPT):
            pf = p["pa"] if dt_ % 2 == 0 else p["pb"]
            if dt_ >= 2:
                self.W("tensor", "sv", evac[dt_ - 2])
            for kk in range(4):
                self.E("tensor", "matmul", out=pf[:],
                       lhsT=s["wo"][:, kk * 1024 + dt_ * 128:kk * 1024 + (dt_ + 1) * 128],
                       rhs=s["obuf"][:, kk * 512:(kk + 1) * 512],
                       start=(kk == 0), stop=(kk == 3))
            tv = self.close("tensor")
            self.W("vector", "st", tv)
            evac[dt_] = self.E("vector", "tensor_add",
                               out=self.xcol(dt_, c0, WIN), in0=pf[:],
                               in1=self.xcol(dt_, c0, WIN), inc=("sv", 1))

    def emit_ffn(self, half, l):
        s, p, dr, a_ = self.s, self.p, self.dram, self.a
        self.bar()
        self.W("gpsimd", "swc", self.wconv_done)
        for tc in range(NTC):
            c0 = tc * WIN
            mdone = {}
            wld = {}
            tmm = {}
            for hh in range(NHH):
                buf = hh % 2
                psem = "sda" if hh % 2 == 0 else "sdb"
                if tc == 0 and hh < 2:
                    wld[hh] = self.ffn_pre[hh]
                else:
                    if hh >= 2:
                        self.W("gpsimd", "st", tmm[hh - 2])
                    else:
                        self.W("gpsimd", "st", self.ffn_prev1[buf])
                    wld[hh] = self.E("gpsimd", "dma_start",
                           out=s["w1c"][:, buf * 2048:(buf + 1) * 2048]
                               .rearrange("p (c n) -> p c n", c=8),
                           in_=dr[f"w1pb{l}"][:, hh * 256:(hh + 1) * 256]
                               .rearrange("(c p) n -> p c n", p=128),
                           inc=(psem, 16))
                pa_ = p["pa"] if hh % 2 == 0 else p["pc"]
                pg_ = p["pb"] if hh % 2 == 0 else p["pd"]
                self.W("tensor", psem, wld[hh])
                if hh >= 2:
                    self.W("tensor", "sv", mdone[hh - 2])
                for kk in range(NPT):
                    self.E("tensor", "matmul", out=pa_[:],
                           lhsT=s["w1c"][:, buf * 2048 + kk * 256:
                                         buf * 2048 + kk * 256 + 128],
                           rhs=self.xbcol(kk, c0, WIN),
                           start=(kk == 0), stop=(kk == NPT - 1))
                for kk in range(NPT):
                    self.E("tensor", "matmul", out=pg_[:],
                           lhsT=s["w1c"][:, buf * 2048 + kk * 256 + 128:
                                         buf * 2048 + kk * 256 + 256],
                           rhs=self.xbcol(kk, c0, WIN),
                           start=(kk == 0), stop=(kk == NPT - 1))
                tv = self.close("tensor")
                tmm[hh] = tv
                self.W("scalar", "st", tv)
                sv = self.E("scalar", "activation", out=a_["gelu"], in_=pg_[:],
                            func=AF.Gelu,
                            bias=s[f"b1p{l}"][:, 2 * hh + 1:2 * hh + 2],
                            inc=("ss", 1))
                self.W("vector", "ss", sv)
                mdone[hh] = self.E("vector", "scalar_tensor_tensor",
                       out=a_["msb"][:, hh * 512:(hh + 1) * 512],
                       in0=pa_[:], scalar=s[f"b1p{l}"][:, 2 * hh:2 * hh + 1],
                       in1=a_["gelu"], op0=ALU.add, op1=ALU.mult, inc=("sv", 1))
            self.ffn_prev1 = {0: tmm[NHH - 2], 1: tmm[NHH - 1]}
            self.W("tensor", "sv", self.cnt("sv"))
            banks = [p["pa"], p["pb"], p["pc"], p["pd"]]
            for dgrp in range(2):
                tmm2 = {}
                for hh in range(NHH):
                    buf = hh % 2
                    if hh >= 2:
                        self.W("gpsimd", "st", tmm2[hh - 2])
                    else:
                        self.W("gpsimd", "st", self.ffn_prev2[buf])
                    psem = "sda" if hh % 2 == 0 else "sdb"
                    wl = self.E("gpsimd", "dma_start",
                           out=s["w2c"][:, buf * 1024:(buf + 1) * 1024],
                           in_=dr[f"w2pb{l}"][hh * 128:(hh + 1) * 128, :],
                           inc=(psem, 16))
                    self.W("tensor", psem, wl)
                    for di in range(4):
                        d = dgrp * 4 + di
                        self.E("tensor", "matmul", out=banks[di][:],
                               lhsT=s["w2c"][:, buf * 1024 + d * 128:
                                             buf * 1024 + (d + 1) * 128],
                               rhs=a_["msb"][:, hh * 512:(hh + 1) * 512],
                               start=(hh == 0), stop=(hh == NHH - 1))
                    tmm2[hh] = self.close("tensor")
                tv = self.close("tensor")
                self.W("vector", "st", tv)
                self.ffn_prev2 = {0: tmm2[NHH - 2], 1: tmm2[NHH - 1]}
                for di in range(4):
                    d = dgrp * 4 + di
                    self.E("vector", "scalar_tensor_tensor",
                           out=self.xcol(d, c0, WIN), in0=banks[di][:],
                           scalar=s[f"b2c{l}"][:, d:d + 1],
                           in1=self.xcol(d, c0, WIN), op0=ALU.add, op1=ALU.add,
                           inc=("sv", 1))
                self.W("tensor", "sv", self.cnt("sv"))

    def emit_final(self, half):
        s, p, dr, a_ = self.s, self.p, self.dram, self.a
        dmas = {}
        adds = {}
        PS = ("sfa", "sfb", "sfc", "sfd")
        for i in range(4):
            kk, c = i // 4, i % 4
            dmas[i] = self.E("sync", "dma_start",
                   out=a_["x1buf"][:, i * 512:(i + 1) * 512],
                   in_=dr["x1s"][kk * 128:(kk + 1) * 128,
                                 half * HALF + c * WIN:half * HALF + (c + 1) * WIN],
                   inc=(PS[i], 16))
        self.bar()
        for i in range(32):
            kk, c = i // 4, i % 4
            buf = i % 4
            if i >= 4:
                self.W("sync", "sv", adds[i - 4])
                dmas[i] = self.E("sync", "dma_start",
                       out=a_["x1buf"][:, buf * 512:(buf + 1) * 512],
                       in_=dr["x1s"][kk * 128:(kk + 1) * 128,
                                     half * HALF + c * WIN:half * HALF + (c + 1) * WIN],
                       inc=(PS[buf], 16))
            self.W("vector", PS[buf], dmas[i])
            adds[i] = self.E("vector", "tensor_add",
                   out=self.xcol(kk, c * WIN, WIN),
                   in0=self.xcol(kk, c * WIN, WIN),
                   in1=a_["x1buf"][:, buf * 512:(buf + 1) * 512], inc=("sv", 1))
        self.emit_ln(l=None, shifted=False, write_xb=False)
        for kk in range(NPT):
            self.E("sync", "dma_start",
                   out=dr["out"][kk * 128:(kk + 1) * 128,
                                 half * HALF:(half + 1) * HALF],
                   in_=self.xcol(kk, 0, HALF), inc=("sst", 16))

    def emit_taps(self):
        pass

    def mtap(self, name, ap, shape, dt=F32):
        if name not in self.taps:
            return
        t = self.tap(name, shape, dt)
        self.bar()
        self.E("sync", "dma_start", out=t, in_=ap, inc=("sst", 16))
        self.bar()

    def finish(self):
        self.bar()


def build_kernel(taps=(), stop_after=None):
    k = Kern(taps=taps, stop_after=stop_after)
    nc = k.build()
    return k, nc


def host_constants():
    c = {}
    c["identb"] = np.eye(128, dtype=ml_dtypes.bfloat16)
    c["onesf"] = np.ones((1, 128), np.float32)
    c["onesb"] = np.ones((1, 128), ml_dtypes.bfloat16)
    c["onesbb"] = np.ones((128, 128), ml_dtypes.bfloat16)
    c["onescb"] = np.ones((128, 1), ml_dtypes.bfloat16)
    c["epscol"] = np.full((128, 1), 1e-5, np.float32)
    c["eps6col"] = np.full((128, 1), 1e-6, np.float32)
    c["pihalf"] = np.full((128, 1), PI / 2, np.float32)
    jj, ii = np.meshgrid(np.arange(128), np.arange(128), indexing="ij")
    c["maskl"] = np.where(jj < ii, -1e30, 0.0).astype(ml_dtypes.bfloat16)
    c["maskr"] = np.where(jj > ii, -1e30, 0.0).astype(ml_dtypes.bfloat16)
    P = np.zeros((128, 128), np.float32)
    for b in range(2):
        for e in range(64):
            P[b * 64 + (e + 32) % 64, b * 64 + e] = 1.0
    c["rotp"] = P.astype(ml_dtypes.bfloat16)
    S = np.zeros((16, 8 * 128), np.float32)
    for a in range(8):
        for p_ in range(128):
            S[2 * a + p_ // 64, a * 128 + p_] = 1.0
    c["selck"] = S.astype(ml_dtypes.bfloat16)
    S = np.zeros((128, 8 * 16), np.float32)
    for f in range(8):
        for k_ in range(128):
            r = 2 * f + k_ // 64
            S[k_, f * 16 + r] = 1.0
    c["segsel"] = S.astype(ml_dtypes.bfloat16)
    Dh, W_ = 64, 512
    half = np.arange(0, Dh, 2, dtype=np.float64)
    inv = 1.0 / 10000.0 ** (half / Dh)
    t = np.arange(T, dtype=np.float64)
    freqs = np.concatenate([t[:, None] * inv[None, :]] * 2, -1)
    base = (half + 0.4 * Dh) / (1.4 * Dh)
    power = (t - T // 2) / (W_ // 2)
    scale = np.concatenate([base[None, :] ** power[:, None]] * 2, -1)
    cosf, sinf = np.cos(freqs), np.sin(freqs)
    sgn = np.concatenate([-np.ones(32), np.ones(32)])
    dup = lambda M: np.concatenate([M, M], axis=0).astype(ml_dtypes.bfloat16)
    c["tcs"] = dup((8.0 * cosf * scale).T)
    c["tss"] = dup((8.0 * sinf * scale * sgn[None, :]).T)
    c["tci"] = dup((cosf / scale).T)
    c["tsi"] = dup((sinf / scale * sgn[None, :]).T)
    return c


def host_core_inputs(ins, core):
    d = {}
    q = np.asarray(ins["L0_attn_query"][core], np.float32)
    d["x0t"] = np.ascontiguousarray(q.T)
    WQ = np.asarray(ins["L0_attn_W_Q"], np.float32)[:, :, 0].T
    WK = np.asarray(ins["L0_attn_W_K"], np.float32)[:, :, 0].T
    wqk = np.zeros((D, 64), np.float32)
    wqk[:, 0:16] = WQ
    wqk[:, 32:48] = WK
    d["wqk"] = wqk
    WV = np.asarray(ins["L0_attn_W_V"], np.float32)
    d["wv"] = np.ascontiguousarray(WV.transpose(1, 0, 2).reshape(D, 1024))
    WO = np.asarray(ins["L0_attn_W_O"], np.float32)
    d["wop"] = np.ascontiguousarray(WO.transpose(2, 1, 0).reshape(1024, D))
    for l in range(2):
        d[f"wqkv{l}"] = np.ascontiguousarray(np.asarray(ins["L0_ffn_wqkv"][l], np.float32))
        d[f"wo{l}"] = np.ascontiguousarray(np.asarray(ins["L0_ffn_wo"][l], np.float32))
        w1 = np.asarray(ins["L0_ffn_w1"][l], np.float32)
        a = np.zeros((D, HID), np.float32); a[:, :2730] = w1[:, :2730]
        g = np.zeros((D, HID), np.float32); g[:, :2730] = w1[:, 2730:]
        w1p = np.zeros((D, NHH, 2, 128), np.float32)
        for hh in range(NHH):
            w1p[:, hh, 0] = a[:, hh * 128:(hh + 1) * 128]
            w1p[:, hh, 1] = g[:, hh * 128:(hh + 1) * 128]
        d[f"w1p{l}"] = np.ascontiguousarray(w1p.reshape(D, 2 * HID))
        w2 = np.asarray(ins["L0_ffn_w2"][l], np.float32)
        w2p = np.zeros((HID, D), np.float32); w2p[:2730] = w2
        d[f"w2p{l}"] = w2p
        b1 = np.asarray(ins["L0_ffn_b1"][l], np.float32)
        b1a = np.zeros(HID, np.float32); b1a[:2730] = b1[:2730]
        b1g = np.zeros(HID, np.float32); b1g[:2730] = b1[2730:]
        b1p = np.zeros((128, 2 * NHH), np.float32)
        for hh in range(NHH):
            b1p[:, 2 * hh] = b1a[hh * 128:(hh + 1) * 128]
            b1p[:, 2 * hh + 1] = b1g[hh * 128:(hh + 1) * 128]
        d[f"b1p{l}"] = b1p
        d[f"b2c{l}"] = np.ascontiguousarray(
            np.asarray(ins["L0_ffn_b2"][l], np.float32).reshape(NPT, 128).T)
        d[f"lng{l}"] = np.ascontiguousarray(
            np.asarray(ins["L0_ffn_ln_g"][l], np.float32).reshape(NPT, 128).T)
        d[f"lnb{l}"] = np.ascontiguousarray(
            np.asarray(ins["L0_ffn_ln_b"][l], np.float32).reshape(NPT, 128).T)
        qs = np.asarray(ins["L0_ffn_q_scale"][l], np.float32)
        ks = np.asarray(ins["L0_ffn_k_scale"][l], np.float32)
        d[f"qs{l}"] = np.ascontiguousarray(np.concatenate([qs, qs]).reshape(128, 1))
        d[f"ks{l}"] = np.ascontiguousarray(np.concatenate([ks, ks]).reshape(128, 1))
    return d


from concourse.bass_utils import run_bass_kernel_spmd


def kernel(**inputs):
    consts = host_constants()
    k = Kern(taps=(), stop_after=None)
    nc = k.build()
    valid = set(k.dram.keys())
    in_maps = []
    for core in range(8):
        cin = host_core_inputs(inputs, core)
        m = {n: np.asarray(v) for n, v in cin.items() if n in valid}
        m.update({n: np.asarray(v) for n, v in consts.items() if n in valid})
        in_maps.append(m)
    res = run_bass_kernel_spmd(nc, in_maps, core_ids=list(range(8)))
    out = np.stack([np.ascontiguousarray(r["out"].T) for r in res.results], axis=0)
    return out.astype(np.float32)


# version 11 (final)
# speedup vs baseline: 1.2189x; 1.0063x over previous
import os
import numpy as np
import ml_dtypes
from contextlib import ExitStack
import concourse.bass as bass
import concourse.mybir as mybir

F32 = mybir.dt.float32
BF16 = mybir.dt.bfloat16
AF = mybir.ActivationFunctionType
ALU = mybir.AluOpType
AX = mybir.AxisListType.X

T, HALF, D = 4096, 2048, 1024
NPT = 8
NTC = 4
WIN = 512
HID = 2816
NHH = 22
PI = float(np.pi)
ENGINES = ("sync", "vector", "scalar", "tensor", "gpsimd")


class Op:
    __slots__ = ("fn", "inc")
    def __init__(self, fn):
        self.fn = fn
        self.inc = None


class Kern:
    def __init__(self, taps=(), stop_after=None):
        self.taps = set(taps)
        self.stop_after = stop_after
        self.nc = bass.Bass("TRN2", target_bir_lowering=False, debug=False, detect_race_conditions=(os.environ.get("RACED", "0") == "1"))
        self.q = {e: [] for e in ENGINES}
        self.sems = {}
        self.es = ExitStack()
        self.dram = {}
        self.tap_tensors = {}
        self._dirty = {e: None for e in ENGINES}
        self.engsem = {"sync": "sx", "vector": "sv", "scalar": "ss",
                       "tensor": "st", "gpsimd": "sw"}

    def sem(self, name):
        h = self.es.enter_context(self.nc.semaphore(name))
        self.sems[name] = [h, 0]

    def E(self, eng, method, *args, inc=None, **kw):
        op = Op(None)
        sems = self.sems
        def fn(e, method=method, args=args, kw=kw, op=op):
            inst = getattr(e, method)(*args, **kw)
            if op.inc is not None:
                inst.then_inc(sems[op.inc[0]][0], op.inc[1])
        op.fn = fn
        self.q[eng].append(op)
        if inc is not None:
            op.inc = inc
            sems[inc[0]][1] += inc[1]
            self._dirty[eng] = None
            return sems[inc[0]][1]
        self._dirty[eng] = op
        return None

    def W(self, eng, semname, val):
        if val <= 0:
            return
        h = self.sems[semname][0]
        self.q[eng].append(Op(lambda e, h=h, v=val: e.wait_ge(h, v)))

    def cnt(self, name):
        return self.sems[name][1]

    def close(self, eng, n=1):
        op = self._dirty[eng]
        nm = self.engsem[eng]
        if op is not None and op.inc is None:
            op.inc = (nm, n)
            self.sems[nm][1] += n
            self._dirty[eng] = None
        return self.sems[nm][1]

    def bar(self, engines=ENGINES):
        for e in ENGINES:
            self.close(e)
        if not hasattr(self, "_lastbar"):
            self._lastbar = {}
        vals = []
        for n in self.sems:
            if n == "swc":
                continue
            v = self.cnt(n)
            if self._lastbar.get(n) != v:
                vals.append((n, v))
                if engines is ENGINES:
                    self._lastbar[n] = v
        for e in engines:
            for n, v in vals:
                self.W(e, n, v)

    def declare_dram(self):
        nc, dr = self.nc, self.dram
        def di(name, shape, dt=F32):
            dr[name] = nc.dram_tensor(name, shape, dt, kind="ExternalInput").ap()
        di("x0t", [D, T])
        di("wqk", [D, 64]); di("wv", [D, 1024]); di("wop", [1024, D])
        for l in range(2):
            di(f"wqkv{l}", [D, 1536]); di(f"wo{l}", [512, D])
            di(f"w1p{l}", [D, 2 * HID]); di(f"w2p{l}", [HID, D])
            di(f"b1p{l}", [128, 2 * NHH]); di(f"b2c{l}", [128, NPT])
            di(f"lng{l}", [128, NPT]); di(f"lnb{l}", [128, NPT])
            di(f"qs{l}", [128, 1]); di(f"ks{l}", [128, 1])
        for n, sh in (("onesf", [1, 128]), ("epscol", [128, 1]),
                      ("eps6col", [128, 1]), ("pihalf", [128, 1])):
            di(n, sh, F32)
        for n, sh in (("maskl", [128, 128]), ("maskr", [128, 128]),
                      ("onesbb", [128, 128]),
                      ("rotp", [128, 128]), ("selck", [16, 8 * 128]),
                      ("segsel", [128, 8 * 16]), ("identb", [128, 128]),
                      ("onesb", [1, 128]), ("onescb", [128, 1])):
            di(n, sh, BF16)
        for n in ("tcs", "tss", "tci", "tsi"):
            di(n, [128, T], BF16)
        dr["out"] = nc.dram_tensor("out", [D, T], F32, kind="ExternalOutput").ap()
        dr["x1s"] = nc.dram_tensor("x1s", [D, T], F32, kind="Internal").ap()
        for l in range(2):
            dr[f"w1pb{l}"] = nc.dram_tensor(f"w1pb{l}", [D, 2 * HID], BF16,
                                            kind="Internal").ap()
            dr[f"w2pb{l}"] = nc.dram_tensor(f"w2pb{l}", [HID, D], BF16,
                                            kind="Internal").ap()
            dr[f"wqkvb{l}"] = nc.dram_tensor(f"wqkvb{l}", [D, 1536], BF16,
                                             kind="Internal").ap()
            dr[f"wob{l}"] = nc.dram_tensor(f"wob{l}", [512, D], BF16,
                                           kind="Internal").ap()

    def tap(self, name, shape, dt=F32):
        if name in self.tap_tensors:
            return self.tap_tensors[name]
        ap = self.nc.dram_tensor("tap_" + name, shape, dt,
                                 kind="ExternalOutput").ap()
        self.tap_tensors[name] = ap
        return ap

    def alloc(self):
        nc, es = self.nc, self.es
        s = {}
        hdl = {}
        def t(name, shape, dt):
            hdl[name] = es.enter_context(nc.sbuf_tensor("s_" + name, shape, dt))
            s[name] = hdl[name][:]
        t("x", [128, NPT * (HALF + 1)], F32)
        t("xb", [128, NPT * (HALF + 1)], BF16)
        t("qbuf", [128, 4 * WIN], BF16)
        for l in range(2):
            t(f"kbuf{l}", [128, 4 * 2 * WIN], BF16)
            t(f"vbuf{l}", [128, 2 * 4 * 8 * 65], BF16)
        t("AR1", [128, 12288], BF16)
        t("AR2", [128, 8192], BF16)
        t("AR3", [128, 4096], BF16)
        t("AR4", [128, 4096], BF16)
        t("AR5", [128, 2048], BF16)
        t("SA", [128, 2048], BF16)
        t("SB", [128, 512], F32)
        t("ROWS", [1, 1024], F32)
        for n, sh, dt in (
            ("onesf", [1, 128], F32), ("epscol", [128, 1], F32),
            ("eps6col", [128, 1], F32), ("pihalf", [128, 1], F32),
            ("maskl", [128, 128], BF16), ("maskr", [128, 128], BF16),
            ("onesbb", [128, 128], BF16),
            ("rotp", [128, 128], BF16), ("selck", [16, 8 * 128], BF16),
            ("segsel", [128, 8 * 16], BF16), ("identb", [128, 128], BF16),
            ("onesb", [1, 128], BF16), ("onescb", [128, 1], BF16),
        ):
            t(n, sh, dt)
        for l in range(2):
            t(f"b1p{l}", [128, 2 * NHH], F32); t(f"b2c{l}", [128, NPT], F32)
            t(f"lng{l}", [128, NPT], F32); t(f"lnb{l}", [128, NPT], F32)
            t(f"qs{l}", [128, 1], F32); t(f"ks{l}", [128, 1], F32)
        t("gxr", [128, 2 * NPT], F32)
        t("gxb", [128, 2 * NPT], BF16)
        t("negm4", [128, 4], BF16)
        t("negmT", [4, 128], BF16)
        t("carry", [128, 24], F32)
        ar1, ar2, ar3 = s["AR1"], s["AR2"], s["AR3"]
        ar4, ar5, sa, sb = s["AR4"], s["AR5"], s["SA"], s["SB"]
        ar1f = ar1.bitcast(F32); ar2f = ar2.bitcast(F32)
        ar3f = ar3.bitcast(F32)
        s["wqkv"] = ar1[:, 0:12288]
        s["heads"] = ar1[:, 0:6144]
        s["wos"] = ar1[:, 6144:10240]
        s["tsk"] = ar1[0:32, 10240:12288]
        s["cumKc"] = ar1f[0:32, 0:2048]
        s["cumKs"] = ar1f[0:32, 2048:4096]
        s["den"] = s["cumKc"]
        s["tcq"] = ar4[0:32, 0:2048]
        s["tsq"] = ar4[0:32, 2048:4096]
        s["cqr"] = s["tcq"]
        s["sqr"] = s["tsq"]
        s["tck"] = s["qbuf"][0:32, 0:2048]
        s["tcs"] = ar2[:, 0:2048]; s["tss"] = ar2[:, 2048:4096]
        s["tci"] = ar2[:, 4096:6144]; s["tsi"] = ar2[:, 6144:8192]
        s["vfm"] = ar2f[:, 0:2048]
        s["cums"] = s["vfm"]
        s["wbld"] = ar2f[:, 2048:4096]
        s["wo"] = ar3[:, 0:4096]
        s["cumc"] = ar3f[:, 0:2048]
        s["w1c"] = ar4[:, 0:4096]
        s["obuf"] = ar4[:, 0:2048]
        s["w2c"] = ar5[:, 0:2048]
        s["wvs"] = ar5[:, 0:2048]
        s["wqk"] = ar5[:, 0:512]
        s["trigt"] = sb[0:64, 0:512]
        s["strow"] = s["ROWS"][0:1, 0:1024]
        s["negrow"] = s["ROWS"].bitcast(BF16)[0:1, 0:512]
        s["rdlrow"] = s["ROWS"][0:1, 512:1024]
        s["onesc"] = sa[:, 0:2048]
        self.s = s
        al = {}
        al["sq2"] = sa[:, 1536:2048]
        al["xc"] = sa[:, 1024:1536]
        al["gelu"] = sb[:, 0:512]
        al["t1"] = sb[:, 0:512]
        al["x1buf"] = ar3f[:, 0:2048]
        al["ssc"] = sb[0:16, 0:512]
        al["rr"] = sa[0:16, 1024:1536]
        al["prepA"] = sa[:, 0:512]
        al["prepB"] = sa[:, 512:1024]
        al["qkraw"] = ar4[:, 0:4096]
        al["ubuf"] = sa[:, 0:1280]
        al["oraw"] = sa[:, 1280:1792]
        al["msb"] = ar1[:, 0:11264]
        self.a = al

    def psums(self):
        nc, es = self.nc, self.es
        p = {}
        for n in ("pa", "pb", "pc", "pd"):
            p[n] = es.enter_context(nc.psum_tensor(n, [128, 512], F32))
        p["pst0"] = es.enter_context(nc.psum_tensor("pst0", [128, 640], F32))
        p["pst1"] = es.enter_context(nc.psum_tensor("pst1", [128, 640], F32))
        p["ptp"] = p["pd"][0:4, 0:64].bitcast(BF16)
        self.p = p

    def xcol(self, kk, tk0, n, shifted=False):
        off = kk * (HALF + 1) + 1 + tk0 - (1 if shifted else 0)
        return self.s["x"][:, off:off + n]

    def xbcol(self, kk, tk0, n, shifted=False):
        off = kk * (HALF + 1) + 1 + tk0 - (1 if shifted else 0)
        return self.s["xb"][:, off:off + n]

    def kap(self, l, f, slot, t0, n):
        return self.s[f"kbuf{l}"][:, f * 1024 + slot * 512 + t0:
                                  f * 1024 + slot * 512 + t0 + n]

    def vap(self, l, slot, tt, h):
        base = slot * (4 * 8 * 65) + tt * (8 * 65) + h * 65
        return self.s[f"vbuf{l}"][:, base:base + 65]

    def build(self):
        for n in ("sx", "sw", "sv", "ss", "st", "smd", "sst", "sda", "sdb",
                  "sdc", "sdd", "sfa", "sfb", "sfc", "sfd", "sme", "swc"):
            self.sem(n)
        self.declare_dram()
        self.alloc()
        self.psums()
        self.emit_const_loads()
        if self.stop_after != "consts":
            self.emit_wconv()
        self.bar()
        done = False
        if self.stop_after == "consts":
            self.emit_taps(); self.finish()
            blk = self.es.enter_context(self.nc.Block())
            for eng in ENGINES:
                ops = self.q[eng]
                def runner(e, ops=ops):
                    for op in ops:
                        op.fn(e)
                getattr(blk, eng)(runner)
            self.es.close()
            return self.nc
        for half in (0, 1):
            self.half = half
            self.emit_x0_load(half)
            if self.stop_after == "x0":
                self.bar()
                break
            self.emit_outer(half)
            if self.stop_after in ("o1", "o2", "o3", "oln"):
                break
            self.bar()
            self.mtap(f"x_outer{half}", self.s["x"],
                      [128, NPT * (HALF + 1)])
            if self.stop_after == f"outer{half}":
                done = True; break
            for l in (0, 1):
                self.emit_layer(half, l)
                self.mtap(f"x_l{half}_{l}", self.s["x"],
                          [128, NPT * (HALF + 1)])
                self.mtap(f"xb_l{half}_{l}", self.s["xb"],
                          [128, NPT * (HALF + 1)], BF16)
                if self.stop_after == f"layer{half}_{l}":
                    done = True; break
            if done:
                break
            self.emit_final(half)
            self.bar()
            self.mtap(f"xfin{half}", self.s["x"], [128, NPT * (HALF + 1)], F32)
            if self.stop_after == f"final{half}":
                done = True; break
        self.emit_taps()
        self.finish()
        blk = self.es.enter_context(self.nc.Block())
        for eng in ENGINES:
            ops = self.q[eng]
            def runner(e, ops=ops):
                for op in ops:
                    op.fn(e)
            getattr(blk, eng)(runner)
        self.es.close()
        return self.nc

    def emit_const_loads(self):
        s, dr = self.s, self.dram
        names = ["onesf", "epscol", "eps6col", "pihalf", "maskl", "maskr",
                 "onesbb",
                 "rotp", "selck", "segsel", "identb", "onesb", "onescb"]
        for l in range(2):
            names += [f"b1p{l}", f"b2c{l}", f"lng{l}", f"lnb{l}", f"qs{l}",
                      f"ks{l}"]
        for n in names:
            self.E("sync", "dma_start", out=s[n], in_=dr[n], inc=("sx", 16))
        v = self.cnt("sx")
        self.W("vector", "sx", v)
        for l in range(2):
            vb = s[f"vbuf{l}"]
            self.E("vector", "memset",
                   vb.rearrange("p (g e) -> p g e", e=65)[:, :, 64:65], 1.0)
        self.E("vector", "memset", s["gxr"], 0.0)
        self.E("vector", "memset", s["gxb"], 0.0)
        self.E("vector", "memset", s["x"], 0.0)
        self.E("vector", "memset", s["xb"], 0.0)
        self.E("vector", "memset", s["carry"], 0.0)

    def emit_wconv(self):
        dr = self.dram
        for l in range(2):
            for (a, b_, rows) in ((f"w1p{l}", f"w1pb{l}", D),
                                  (f"w2p{l}", f"w2pb{l}", HID),
                                  (f"wqkv{l}", f"wqkvb{l}", D),
                                  (f"wo{l}", f"wob{l}", 512)):
                step = rows // 4
                for i in range(4):
                    self.E("gpsimd", "dma_start",
                           out=dr[b_][i * step:(i + 1) * step, :],
                           in_=dr[a][i * step:(i + 1) * step, :],
                           inc=("swc", 16))
        self.wconv_done = self.cnt("swc")

    def emit_x0_load(self, half):
        s, dr = self.s, self.dram
        self.E("gpsimd", "dma_start",
               out=s["wqk"].rearrange("p (a n) -> p a n", a=8),
               in_=dr["wqk"].rearrange("(a p) n -> p a n", p=128),
               inc=("sw", 16))
        self.x0wqk = self.cnt("sw")
        PS = ("sfa", "sfb", "sfc", "sfd")
        self.x0ps = PS
        self.x0g = {}
        for tc in range(NTC):
            for kk in range(NPT):
                self.E("sync", "dma_start",
                       out=self.xcol(kk, tc * WIN, WIN),
                       in_=dr["x0t"][kk * 128:(kk + 1) * 128,
                                     half * HALF + tc * WIN:
                                     half * HALF + (tc + 1) * WIN],
                       inc=(PS[tc], 16))
            self.x0g[tc] = self.cnt(PS[tc])

    def emit_outer(self, half):
        s, p, dr, a_ = self.s, self.p, self.dram, self.a
        self.E("vector", "memset", s["onesc"], 1.0)
        tanh_done = {}
        self.W("tensor", "sw", self.x0wqk)
        for tc in range(NTC):
            pv_ = p["pa"] if tc % 2 == 0 else p["pb"]
            self.W("vector", self.x0ps[tc], self.x0g[tc])
            for kk in range(NPT):
                self.E("vector", "tensor_copy",
                       out=self.xbcol(kk, tc * WIN, WIN),
                       in_=self.xcol(kk, tc * WIN, WIN))
            cast_done = self.close("vector")
            self.W("tensor", "sv", cast_done)
            if tc >= 2:
                self.W("tensor", "ss", tanh_done[tc - 2])
            for kk in range(NPT):
                self.E("tensor", "matmul", out=pv_[0:64, :],
                       lhsT=s["wqk"][:, kk * 64:(kk + 1) * 64],
                       rhs=self.xbcol(kk, tc * WIN, WIN),
                       start=(kk == 0), stop=(kk == NPT - 1))
            tv = self.close("tensor")
            self.W("scalar", "st", tv)
            tanh_done[tc] = self.E("scalar", "activation", out=s["trigt"],
                   in_=pv_[0:64, :], func=AF.Tanh, inc=("ss", 1))
            c0 = tc * WIN
            self.E("scalar", "activation", out=s["tcq"][:, c0:c0 + WIN],
                   in_=s["trigt"][0:32, :], func=AF.Sin,
                   bias=s["pihalf"][0:32, :], scale=PI / 4)
            self.E("scalar", "activation", out=s["tsq"][:, c0:c0 + WIN],
                   in_=s["trigt"][0:32, :], func=AF.Sin, scale=PI / 4)
            self.E("scalar", "activation", out=s["tck"][:, c0:c0 + WIN],
                   in_=s["trigt"][32:64, :], func=AF.Sin,
                   bias=s["pihalf"][0:32, :], scale=PI / 4)
            self.E("scalar", "activation", out=s["tsk"][:, c0:c0 + WIN],
                   in_=s["trigt"][32:64, :], func=AF.Sin, scale=PI / 4,
                   inc=("ss", 1))
        if self.stop_after == "o1":
            return
        sv = self.close("scalar")
        self.W("vector", "ss", sv)
        ca = s["carry"]
        for (src, dst, cc) in ((s["tck"], s["cumKc"], 16),
                               (s["tsk"], s["cumKs"], 17)):
            init = ca[0:32, cc:cc + 1] if half == 1 else 0.0
            vv = self.E("vector", "tensor_tensor_scan", out=dst[:],
                        data0=s["onesc"][0:32, :], data1=src[:], initial=init,
                        op0=ALU.mult, op1=ALU.add, inc=("sv", 1))
            self.W("vector", "sv", vv)
            self.E("vector", "tensor_copy", out=ca[0:32, cc:cc + 1],
                   in_=dst[:, HALF - 1:HALF])
        self.E("vector", "tensor_mul", out=s["den"], in0=s["tcq"],
               in1=s["cumKc"])
        self.E("vector", "tensor_mul", out=s["cumKs"], in0=s["tsq"],
               in1=s["cumKs"])
        vv = self.E("vector", "tensor_add", out=s["den"], in0=s["den"],
                    in1=s["cumKs"], inc=("sv", 1))
        self.W("scalar", "sv", vv)
        self.E("scalar", "activation", out=s["den"], in_=s["den"],
               func=AF.Ln, bias=s["eps6col"][0:32, :])
        sv = self.E("scalar", "activation", out=s["den"], in_=s["den"],
                    func=AF.Exp, scale=-1.0, inc=("ss", 1))
        self.W("vector", "ss", sv)
        self.E("vector", "tensor_mul", out=s["cqr"], in0=s["tcq"],
               in1=s["den"])
        self.E("vector", "tensor_mul", out=s["sqr"], in0=s["tsq"],
               in1=s["den"])
        self.bar()
        self.mtap(f"caro{half}", self.s["carry"], [128, 24], F32)
        self.mtap("cumk", self.s["AR1"][0:32, 0:8192], [32, 8192], BF16)
        self.mtap("tcq", self.s["AR4"][0:32, 0:2048], [32, 2048], BF16)
        self.mtap("tsq", self.s["AR4"][0:32, 2048:4096], [32, 2048], BF16)
        self.mtap("tck", self.s["qbuf"][0:32, 0:2048], [32, 2048], BF16)
        self.mtap("tsk", self.s["AR1"][0:32, 10240:12288], [32, 2048], BF16)
        self.mtap("rd", self.s["AR1"][0:32, 0:4096], [32, 4096], BF16)
        if self.stop_after == "o2":
            return
        self.vproj_t = {}
        for (g0, gn) in ((0, 3), (3, 3), (6, 2)):
            self.W("gpsimd", "st", self.cnt("st"))
            wos_done = []
            for ai in range(gn):
                a = g0 + ai
                self.E("gpsimd", "dma_start",
                       out=s["wos"][:, ai * 1024:(ai + 1) * 1024],
                       in_=dr["wop"][a * 128:(a + 1) * 128, :], inc=("sw", 16))
                wos_done.append(self.cnt("sw"))
            for ai in range(gn):
                a = g0 + ai
                wb = ai % 2
                wv_done = self.E("gpsimd", "dma_start",
                       out=s["wvs"][:, wb * 1024:(wb + 1) * 1024]
                           .rearrange("p (c n) -> p c n", c=8),
                       in_=dr["wv"][:, a * 128:(a + 1) * 128]
                           .rearrange("(c p) n -> p c n", p=128),
                       inc=("sw", 16))
                evacs = {}
                self.W("tensor", "sv", self.cnt("sv"))
                for tc in range(NTC):
                    pv = p["pa"] if tc % 2 == 0 else p["pb"]
                    self.W("tensor", "sw", wv_done)
                    if tc >= 2:
                        self.W("tensor", "sv", evacs[tc - 2])
                    for kk in range(NPT):
                        self.E("tensor", "matmul", out=pv[:],
                               lhsT=s["wvs"][:, wb * 1024 + kk * 128:
                                             wb * 1024 + (kk + 1) * 128],
                               rhs=self.xbcol(kk, tc * WIN, WIN),
                               start=(kk == 0), stop=(kk == NPT - 1))
                    tv = self.close("tensor")
                    self.vproj_t[a] = tv
                    self.W("vector", "st", tv)
                    evacs[tc] = self.E("vector", "tensor_copy",
                                       out=s["vfm"][:, tc * WIN:(tc + 1) * WIN],
                                       in_=pv[:], inc=("sv", 1))
                if os.environ.get("O3LIM") == "vproj":
                    continue
                self.bar()
                for (tt_, cum, cc) in ((s["tck"], s["cumc"], 0),
                                       (s["tsk"], s["cums"], 8)):
                    self.W("tensor", "sv", self.cnt("sv"))
                    for tc in range(NTC):
                        pcb = p["pc"] if tc % 2 == 0 else p["pd"]
                        if tc >= 2:
                            self.W("tensor", "sv", self.cnt("sv"))
                        if os.environ.get("CKMODE") == "id":
                            self.E("tensor", "matmul", out=pcb[:],
                                   lhsT=s["identb"],
                                   rhs=s["onesc"][:, 0:512],
                                   start=True, stop=True)
                        else:
                            self.E("tensor", "matmul", out=pcb[:],
                                   lhsT=s["selck"][:, a * 128:(a + 1) * 128],
                                   rhs=tt_[0:16, tc * WIN:(tc + 1) * WIN],
                                   start=True, stop=True)
                        tv = self.close("tensor")
                        self.W("vector", "st", tv)
                        wmode = os.environ.get("WMODE", "")
                        if wmode == "copy":
                            self.E("vector", "tensor_copy",
                                   out=s["wbld"][:, tc * WIN:(tc + 1) * WIN],
                                   in_=pcb[:], inc=("sv", 1))
                        elif wmode == "novfm":
                            self.E("vector", "tensor_mul",
                                   out=s["wbld"][:, tc * WIN:(tc + 1) * WIN],
                                   in0=pcb[:], in1=pcb[:], inc=("sv", 1))
                        else:
                            self.E("vector", "tensor_mul",
                                   out=s["wbld"][:, tc * WIN:(tc + 1) * WIN],
                                   in0=pcb[:],
                                   in1=s["vfm"][:, tc * WIN:(tc + 1) * WIN],
                                   inc=("sv", 1))
                    init = ca[:, cc + a:cc + a + 1] if half == 1 else 0.0
                    vv = self.E("vector", "tensor_tensor_scan", out=cum[:],
                                data0=s["onesc"], data1=s["wbld"],
                                initial=init, op0=ALU.mult, op1=ALU.add,
                                inc=("sv", 1))
                    self.W("vector", "sv", vv)
                    self.E("vector", "tensor_copy",
                           out=ca[:, cc + a:cc + a + 1],
                           in_=cum[:, HALF - 1:HALF], inc=("sv", 1))
                if os.environ.get("O3LIM") == "wbuild":
                    continue
                self.bar()
                scans_done = self.cnt("sv")
                for tc in range(NTC):
                    pq = p["pc"] if tc % 2 == 0 else p["pd"]
                    ps_ = p["pa"] if tc % 2 == 0 else p["pb"]
                    self.W("tensor", "sv", scans_done)
                    self.E("tensor", "matmul", out=pq[:],
                           lhsT=s["selck"][:, a * 128:(a + 1) * 128],
                           rhs=s["cqr"][0:16, tc * WIN:(tc + 1) * WIN],
                           start=True, stop=True)
                    self.E("tensor", "matmul", out=ps_[:],
                           lhsT=s["selck"][:, a * 128:(a + 1) * 128],
                           rhs=s["sqr"][0:16, tc * WIN:(tc + 1) * WIN],
                           start=True, stop=True)
                    tv = self.close("tensor")
                    self.W("vector", "st", tv)
                    h0 = ai * HALF + tc * WIN
                    self.E("vector", "tensor_mul", out=s["wbld"][:, 0:WIN],
                           in0=pq[:], in1=s["cumc"][:, tc * WIN:(tc + 1) * WIN])
                    self.E("vector", "tensor_mul", out=s["wbld"][:, WIN:2 * WIN],
                           in0=ps_[:], in1=s["cums"][:, tc * WIN:(tc + 1) * WIN])
                    self.E("vector", "tensor_add",
                           out=s["heads"][:, h0:h0 + WIN],
                           in0=s["wbld"][:, 0:WIN], in1=s["wbld"][:, WIN:2 * WIN],
                           inc=("sv", 1))
                    self.W("tensor", "sv", self.cnt("sv"))
                self.bar()
                if a == int(os.environ.get("TAPA", "-1")):
                    self.mtap("headsA",
                              self.s["heads"][:, ai * HALF:(ai + 1) * HALF],
                              [128, HALF], BF16)
            if g0 == 0:
                self.bar()
                self.mtap("heads0", self.s["AR1"][:, 0:6144], [128, 6144], BF16)
            if os.environ.get("O3LIM") in ("vproj", "wbuild", "heads"):
                self.bar()
                continue
            heads_done = self.close("vector")
            self.W("tensor", "sv", heads_done)
            for ai in range(gn):
                self.W("tensor", "sw", wos_done[ai])
            evacs = {}
            for tc in range(NTC):
                for dt_ in range(NPT):
                    i = tc * NPT + dt_
                    pd_ = p["pa"] if i % 2 == 0 else p["pb"]
                    if i >= 2:
                        self.W("tensor", "sv", evacs[i - 2])
                    for ai in range(gn):
                        self.E("tensor", "matmul", out=pd_[:],
                               lhsT=s["wos"][:, ai * 1024 + dt_ * 128:
                                             ai * 1024 + (dt_ + 1) * 128],
                               rhs=s["heads"][:, ai * HALF + tc * WIN:
                                              ai * HALF + (tc + 1) * WIN],
                               start=(ai == 0), stop=(ai == gn - 1))
                    tv = self.close("tensor")
                    self.W("vector", "st", tv)
                    evacs[i] = self.E("vector", "tensor_add",
                                      out=self.xcol(dt_, tc * WIN, WIN),
                                      in0=pd_[:],
                                      in1=self.xcol(dt_, tc * WIN, WIN),
                                      inc=("sv", 1))
            self.bar()
        self.mtap("xpre", self.s["x"], [128, NPT * (HALF + 1)], F32)
        if self.stop_after == "o3":
            return
        self.emit_ln(l=None, shifted=False)
        for kk in range(NPT):
            self.E("sync", "dma_start",
                   out=dr["x1s"][kk * 128:(kk + 1) * 128,
                                 half * HALF:(half + 1) * HALF],
                   in_=self.xcol(kk, 0, HALF), inc=("sx", 16))
        if self.half == 0:
            self.save_gxb(0)
        v = self.cnt("st")
        self.ffn_prev1 = {0: v, 1: v}
        self.ffn_prev2 = {0: v, 1: v}

    def emit_ln(self, l, shifted, write_xb=True):
        s, p, a_ = self.s, self.p, self.a
        self.bar()
        xcb = [a_["xc"], a_["prepA"]]
        sqb = [a_["sq2"], a_["prepB"]]
        norm_done = {-2: 0, -1: 0}
        chain_ss = {}
        self.stmm = {}

        def pair(tc):
            return (p["pc"], p["pd"]) if tc % 2 == 0 else (p["pa"], p["pb"])

        def stats(tc):
            sp, sq = pair(tc)
            if tc >= 2:
                self.W("tensor", "sv", norm_done[tc - 2])
            for kk in range(NPT):
                i = tc * NPT + kk
                b = i % 2
                sh = shifted and kk >= 4
                xa = self.xcol(kk, tc * WIN, WIN, shifted=sh)
                if i >= 2:
                    self.W("vector", "st", self.stmm[i - 2])
                    self.W("scalar", "st", self.stmm[i - 2])
                if kk % 2 == 0:
                    self.E("vector", "tensor_copy", out=xcb[b], in_=xa)
                    vv = self.close("vector")
                    sv = self.E("scalar", "activation", out=sqb[b], in_=xa,
                                func=AF.Square, inc=("ss", 1))
                else:
                    sv = self.E("scalar", "activation", out=xcb[b], in_=xa,
                                func=AF.Identity, inc=("ss", 1))
                    self.E("vector", "tensor_mul", out=sqb[b], in0=xa, in1=xa)
                    vv = self.close("vector")
                self.W("tensor", "sv", vv)
                self.W("tensor", "ss", sv)
                self.E("tensor", "matmul", out=sp[:], lhsT=s["onesbb"],
                       rhs=xcb[b], start=(kk == 0), stop=(kk == NPT - 1))
                tv = self.E("tensor", "matmul", out=sq[:], lhsT=s["onesbb"],
                            rhs=sqb[b], start=(kk == 0), stop=(kk == NPT - 1),
                            inc=("st", 1))
                self.stmm[i] = tv

        def chain(tc):
            sp, sq = pair(tc)
            self.W("vector", "st", self.stmm[tc * NPT + NPT - 1])
            vv = self.E("vector", "tensor_scalar", out=sp[:], in0=sp[:],
                        scalar1=1.0 / 1024.0, scalar2=None, op0=ALU.mult,
                        inc=("sv", 1))
            self.W("scalar", "sv", vv)
            sv = self.E("scalar", "activation", out=a_["xc"], in_=sp[:],
                        func=AF.Square, inc=("ss", 1))
            self.E("vector", "tensor_scalar", out=sq[:], in0=sq[:],
                   scalar1=1.0 / 1024.0, scalar2=None, op0=ALU.mult)
            self.W("vector", "ss", sv)
            vv = self.E("vector", "tensor_sub", out=sq[:], in0=sq[:],
                        in1=a_["xc"], inc=("sv", 1))
            self.W("scalar", "sv", vv)
            self.E("scalar", "activation", out=sq[:], in_=sq[:], func=AF.Ln,
                   bias=s["epscol"])
            sv = self.E("scalar", "activation", out=sq[:], in_=sq[:],
                        func=AF.Exp, scale=-0.5, inc=("ss", 1))
            chain_ss[tc] = sv

        def norm(tc):
            sp, sq = pair(tc)
            self.W("vector", "ss", chain_ss[tc])
            for kk in range(NPT):
                sh = shifted and kk >= 4
                xa = self.xcol(kk, tc * WIN, WIN, shifted=sh)
                if l is not None:
                    self.W("vector", "ss", self.cnt("ss"))
                self.E("vector", "tensor_sub", out=a_["t1"], in0=xa,
                       in1=sp[:])
                if l is None:
                    if write_xb:
                        vv = self.E("vector", "tensor_mul", out=xa,
                                    in0=a_["t1"], in1=sq[:], inc=("sv", 1))
                        self.W("scalar", "sv", vv)
                        self.E("scalar", "activation",
                               out=self.xbcol(kk, tc * WIN, WIN), in_=xa,
                               func=AF.Identity, inc=("ss", 1))
                    else:
                        self.E("vector", "tensor_mul", out=xa, in0=a_["t1"],
                               in1=sq[:])
                else:
                    vv = self.E("vector", "tensor_mul", out=a_["t1"],
                                in0=a_["t1"], in1=sq[:], inc=("sv", 1))
                    self.W("scalar", "sv", vv)
                    self.E("scalar", "activation",
                           out=self.xbcol(kk, tc * WIN, WIN), in_=a_["t1"],
                           func=AF.Identity, scale=s[f"lng{l}"][:, kk:kk + 1],
                           bias=s[f"lnb{l}"][:, kk:kk + 1], inc=("ss", 1))
            norm_done[tc] = self.close("vector")
            self.close("scalar")

        for tc in range(NTC):
            stats(tc)
            chain(tc)
            if tc >= 1:
                norm(tc - 1)
        norm(NTC - 1)
        self.bar()

    def save_gxb(self, l):
        s = self.s
        self.E("vector", "tensor_copy", out=s["gxb"][:, l * NPT:(l + 1) * NPT],
               in_=s["xb"].rearrange("p (k c) -> p k c", c=HALF + 1)
                   [:, :, HALF:HALF + 1].rearrange("p k c -> p (k c)"))

    def save_gxr(self, l):
        s = self.s
        self.E("vector", "tensor_copy", out=s["gxr"][:, l * NPT:(l + 1) * NPT],
               in_=s["x"].rearrange("p (k c) -> p k c", c=HALF + 1)
                   [:, :, HALF:HALF + 1].rearrange("p k c -> p (k c)"))

    def load_guards(self, l, which):
        s = self.s
        if which == "x":
            self.E("vector", "tensor_copy",
                   out=s["x"].rearrange("p (k c) -> p k c", c=HALF + 1)
                       [:, :, 0:1].rearrange("p k c -> p (k c)"),
                   in_=s["gxr"][:, l * NPT:(l + 1) * NPT])
        else:
            self.E("vector", "tensor_copy",
                   out=s["xb"].rearrange("p (k c) -> p k c", c=HALF + 1)
                       [:, :, 0:1].rearrange("p k c -> p (k c)"),
                   in_=s["gxb"][:, l * NPT:(l + 1) * NPT])

    def emit_layer(self, half, l):
        s, p, dr, a_ = self.s, self.p, self.dram, self.a
        if l == 1:
            for kk in range(NPT):
                self.E("vector", "tensor_copy", out=self.xbcol(kk, 0, HALF),
                       in_=self.xcol(kk, 0, HALF))
            if half == 0:
                self.save_gxb(1)
        if half == 1:
            self.load_guards(l, "xb")
        NW = os.environ.get("NOWL", "")
        if l == 0 and "t" not in NW:
            for n in ("tcs", "tss", "tci", "tsi"):
                self.E("sync", "dma_start", out=s[n],
                       in_=dr[n][:, half * HALF:(half + 1) * HALF],
                       inc=("sx", 16))
        self.W("gpsimd", "swc", self.wconv_done)
        if "q" not in NW:
            self.E("gpsimd", "dma_start",
                   out=s["wqkv"].rearrange("p (c n) -> p c n", c=8),
                   in_=dr[f"wqkvb{l}"].rearrange("(c p) n -> p c n", p=128),
                   inc=("sw", 16))
        if "o" not in NW:
            self.E("gpsimd", "dma_start",
                   out=s["wo"].rearrange("p (c n) -> p c n", c=4),
                   in_=dr[f"wob{l}"].rearrange("(c p) n -> p c n", p=128),
                   inc=("sw", 16))
        self.bar()
        LL = os.environ.get("LL", "qkv,attn,wo,ln,ffn").split(",")
        for tc in range(NTC):
            self.emit_qkv(half, l, tc)
            self.bar()
            if "attn" not in LL:
                continue
            self.emit_attn(half, l, tc)
            self.bar()
            if "wo" not in LL:
                continue
            self.emit_wo(half, l, tc)
            self.bar()
        self.mtap(f"xwo{half}_{l}", self.s["x"], [128, NPT * (HALF + 1)], F32)
        if "ln" not in LL:
            return
        if half == 0:
            self.save_gxr(l)
        if half == 1:
            self.load_guards(l, "x")
        self.W("gpsimd", "swc", self.wconv_done)
        self.W("gpsimd", "st", self.cnt("st"))
        self.ffn_pre = {}
        for hh in (0, 1):
            psem = "sda" if hh % 2 == 0 else "sdb"
            self.ffn_pre[hh] = self.E("gpsimd", "dma_start",
                   out=s["w1c"][:, hh * 2048:(hh + 1) * 2048]
                       .rearrange("p (c n) -> p c n", c=8),
                   in_=dr[f"w1pb{l}"][:, hh * 256:(hh + 1) * 256]
                       .rearrange("(c p) n -> p c n", p=128),
                   inc=(psem, 16))
        self.emit_ln(l=l, shifted=True)
        self.mtap(f"hn{half}_{l}", self.s["xb"], [128, NPT * (HALF + 1)], BF16)
        if "ffn" not in LL:
            return
        self.emit_ffn(half, l)
        self.bar()

    def emit_qkv(self, half, l, tc):
        s, p, a_ = self.s, self.p, self.a
        slot = (half * NTC + tc) % 2
        c0 = tc * WIN
        qsc = s[f"qs{l}"]; ksc = s[f"ks{l}"]
        evac = {}
        vevac = {}
        for f in range(8):
            pf = p["pa"] if f % 2 == 0 else p["pb"]
            if f >= 2:
                self.W("tensor", "ss", evac[f - 2])
                self.W("tensor", "sv", vevac[f - 2])
            col0 = f * 128 if f < 4 else 512 + (f - 4) * 128
            _sh = os.environ.get("NOSHIFT", "") != "1"
            for kk in range(NPT):
                self.E("tensor", "matmul", out=pf[:],
                       lhsT=s["wqkv"][:, kk * 1536 + col0:kk * 1536 + col0 + 128],
                       rhs=self.xbcol(kk, c0, WIN, shifted=(_sh and kk >= 4)),
                       start=(kk == 0), stop=(kk == NPT - 1))
            tv = self.close("tensor")
            self.W("vector", "st", tv)
            self.W("scalar", "st", tv)
            vevac[f] = self.E("vector", "tensor_copy",
                   out=a_["qkraw"][:, f * 512:(f + 1) * 512], in_=pf[:],
                   inc=("sv", 1))
            self.W("scalar", "sv", vevac[f])
            sv = self.E("scalar", "activation", out=a_["sq2"], in_=pf[:],
                        func=AF.Square, inc=("ss", 1))
            evac[f] = sv
            self.W("tensor", "ss", sv)
            self.E("tensor", "matmul", out=p["pd"][0:16, :],
                   lhsT=s["segsel"][:, f * 16:(f + 1) * 16], rhs=a_["sq2"],
                   start=(f == 0), stop=(f == 7))
            tv = self.close("tensor")
            self.W("scalar", "st", tv)
        segdone = self.cnt("st")
        if os.environ.get("QL") == "qk":
            return
        qkevac = self.cnt("sv")
        vev = {}
        for tt in range(4):
            pf = p["pa"] if tt % 2 == 0 else p["pb"]
            if tt < 2:
                self.W("tensor", "sv", qkevac)
                self.W("tensor", "ss", self.cnt("ss"))
            else:
                self.W("tensor", "sv", vev[tt - 2])
            for kk in range(NPT):
                self.E("tensor", "matmul", out=pf[:],
                       lhsT=self.xbcol(kk, c0 + tt * 128, 128, shifted=(kk >= 4)),
                       rhs=s["wqkv"][:, kk * 1536 + 1024:kk * 1536 + 1536],
                       start=(kk == 0), stop=(kk == NPT - 1))
            tv = self.close("tensor")
            self.W("vector", "st", tv)
            base = slot * (4 * 8 * 65) + tt * (8 * 65)
            out = s[f"vbuf{l}"][:, base:base + 520] \
                .rearrange("p (h e) -> p h e", e=65)[:, :, 0:64]
            vev[tt] = self.E("vector", "tensor_copy", out=out,
                             in_=pf[:].rearrange("p (h e) -> p h e", e=64),
                             inc=("sv", 1))
        if os.environ.get("QL") == "v":
            return
        self.W("vector", "st", segdone)
        vv = self.E("vector", "tensor_scalar", out=a_["ssc"],
                    in0=p["pd"][0:16, :], scalar1=1e-24, scalar2=None,
                    op0=ALU.max, inc=("sv", 1))
        self.W("scalar", "sv", vv)
        self.E("scalar", "activation", out=a_["ssc"], in_=a_["ssc"], func=AF.Ln)
        rrdone = self.E("scalar", "activation", out=a_["rr"], in_=a_["ssc"],
                        func=AF.Exp, scale=-0.5, inc=("ss", 1))
        if os.environ.get("QL") == "stats":
            return
        self.W("tensor", "ss", rrdone)
        self.W("tensor", "sv", self.cnt("sv"))
        for f in range(8):
            prb = p["pc"] if f % 2 == 0 else p["pd"]
            prot = p["pa"] if f % 2 == 0 else p["pb"]
            self.E("tensor", "matmul", out=prb[:],
                   lhsT=s["selck"][:, f * 128:(f + 1) * 128], rhs=a_["rr"],
                   start=True, stop=True)
            tv = self.close("tensor")
            self.W("vector", "st", tv)
            vv = self.E("vector", "scalar_tensor_tensor", out=a_["prepA"],
                        in0=a_["qkraw"][:, f * 512:(f + 1) * 512],
                        scalar=(qsc if f < 4 else ksc)[:],
                        in1=prb[:], op0=ALU.mult, op1=ALU.mult, inc=("sv", 1))
            self.W("tensor", "sv", vv)
            self.E("tensor", "matmul", out=prot[:], lhsT=s["rotp"],
                   rhs=a_["prepA"], start=True, stop=True)
            tv = self.close("tensor")
            self.W("vector", "st", tv)
            tab1 = s["tcs"] if f < 4 else s["tci"]
            tab2 = s["tss"] if f < 4 else s["tsi"]
            self.E("vector", "tensor_mul", out=a_["prepB"], in0=a_["prepA"],
                   in1=tab1[:, c0:c0 + WIN])
            self.E("vector", "tensor_mul", out=a_["prepA"], in0=prot[:],
                   in1=tab2[:, c0:c0 + WIN])
            if f < 4:
                dst = s["qbuf"][:, f * 512:(f + 1) * 512]
            else:
                dst = self.kap(l, f - 4, slot, 0, 512)
            vv = self.E("vector", "tensor_add", out=dst, in0=a_["prepB"],
                        in1=a_["prepA"], inc=("sv", 1))
            self.W("tensor", "sv", vv)

    def emit_attn(self, half, l, tc):
        s, p, a_ = self.s, self.p, self.a
        wg = half * NTC + tc
        first = (wg == 0)
        slot_cur = wg % 2
        slot_prev = 1 - slot_cur
        NR = [s["ROWS"].bitcast(BF16)[0:1, 0:512],
              s["SB"].bitcast(BF16)[0:1, 0:512]]
        SEM = ["smd", "sme"]
        red = {}
        dmad = {}
        obufv = {-1: 0}

        def A_mm(h, it):
            f = h // 2
            pb_ = (h % 2) * 64
            jlo = 512 if first else it * 128
            jhi = it * 128 + 640
            cuts = sorted({jlo, jhi, 512, it * 128 + 512})
            cuts = [c for c in cuts if jlo <= c <= jhi]
            pieces = [(cuts[i], cuts[i + 1]) for i in range(len(cuts) - 1)
                      if cuts[i + 1] > cuts[i]]
            if it >= 1:
                self.W("tensor", "sv", red[h][it - 1])
            bankA = [q for q in pieces if q[0] - it * 128 < 512]
            bankB = [q for q in pieces if q[0] - it * 128 >= 512]
            for q in pieces:
                a0, b0 = q
                sl = slot_prev if a0 < 512 else slot_cur
                t0 = a0 - 512 if a0 >= 512 else a0
                c0_, c1_ = a0 - it * 128, b0 - it * 128
                grp = bankA if c0_ < 512 else bankB
                self.E("tensor", "matmul", out=p["pst0"][:, c0_:c1_],
                       lhsT=s["qbuf"][pb_:pb_ + 64,
                                      f * 512 + it * 128:f * 512 + it * 128 + 128],
                       rhs=self.kap(l, f, sl, t0, b0 - a0)[pb_:pb_ + 64, :],
                       start=(q == grp[0]), stop=(q == grp[-1]))
            tv = self.close("tensor")
            self.W("vector", "st", tv)
            red[h][it] = self.E("vector", "reduce_max",
                                out=s["negm4"][:, it:it + 1],
                                in_=p["pst0"][:, jlo - it * 128:640],
                                axis=AX, negate=True, inc=("sv", 1))

        def B(h):
            self.W("tensor", "sv", red[h][3])
            tv = self.E("tensor", "transpose", out=p["ptp"], in_=s["negm4"],
                        identity=s["identb"], inc=("st", 1))
            self.W("vector", "st", tv)
            vv = self.E("vector", "tensor_copy", out=s["negmT"], in_=p["ptp"],
                        inc=("sv", 1))
            self.W("scalar", "sv", vv)
            dmad[h] = self.E("scalar", "dma_start", out=NR[h % 2],
                             in_=s["negmT"], inc=(SEM[h % 2], 16))

        def C(h, it):
            f = h // 2
            pb_ = (h % 2) * 64
            ss0 = (4 - it) if first else 0
            for sidx in range(ss0, 5):
                g = it + sidx
                sl = slot_prev if g < 4 else slot_cur
                tt_in = g % 4
                self.E("tensor", "matmul",
                       out=p["pst1"][:, sidx * 128:(sidx + 1) * 128],
                       lhsT=self.kap(l, f, sl, tt_in * 128, 128)[pb_:pb_ + 64, :],
                       rhs=s["qbuf"][pb_:pb_ + 64,
                                     f * 512 + it * 128:f * 512 + it * 128 + 128],
                       start=(sidx == ss0 or sidx == 4), stop=False)
            self.W("tensor", SEM[h % 2], dmad[h])
            for sidx in range(ss0, 5):
                stopA = (ss0 > 0 and sidx == 3)
                self.E("tensor", "matmul",
                       out=p["pst1"][:, sidx * 128:(sidx + 1) * 128],
                       lhsT=s["onesb"],
                       rhs=NR[h % 2][0:1, it * 128:(it + 1) * 128],
                       start=False, stop=stopA)
            if ss0 == 0:
                self.E("tensor", "matmul", out=p["pst1"][:, 0:128],
                       lhsT=s["identb"], rhs=s["maskl"],
                       start=False, stop=True)
            tv = self.E("tensor", "matmul", out=p["pst1"][:, 512:640],
                        lhsT=s["identb"], rhs=s["maskr"],
                        start=False, stop=True, inc=("st", 1))
            self.W("scalar", "st", tv)
            n = (5 - ss0) * 128
            ub = a_["ubuf"][:, (it % 2) * 640:(it % 2) * 640 + n]
            sv = self.E("scalar", "activation", out=ub,
                        in_=p["pst1"][:, ss0 * 128:640], func=AF.Exp,
                        inc=("ss", 1))
            return ss0, ub, sv

        def PV(h, it, ss0, ub, sv):
            self.W("tensor", "ss", sv)
            for sidx in range(ss0, 5):
                g = it + sidx
                sl = slot_prev if g < 4 else slot_cur
                tt_in = g % 4
                self.E("tensor", "matmul",
                       out=p["pa"][0:65, it * 128:(it + 1) * 128],
                       lhsT=self.vap(l, sl, tt_in, h),
                       rhs=ub[:, (sidx - ss0) * 128:(sidx - ss0 + 1) * 128],
                       start=(it == 0 and sidx == ss0),
                       stop=(it == 3 and sidx == 4))

        def E_(h):
            f = h // 2
            pb_ = (h % 2) * 64
            pvdone = self.close("tensor")
            self.W("scalar", "st", pvdone)
            self.E("scalar", "activation", out=s["rdlrow"],
                   in_=p["pa"][64:65, :], func=AF.Ln)
            self.E("scalar", "activation", out=s["rdlrow"],
                   in_=s["rdlrow"], func=AF.Exp, scale=-1.0)
            sv = self.E("scalar", "copy", out=a_["oraw"][0:64, :],
                        in_=p["pa"][0:64, :], inc=("ss", 1))
            self.W("tensor", "sv", obufv[h - 1])
            self.W("tensor", "ss", sv)
            tv = self.E("tensor", "matmul", out=p["pc"][0:64, :],
                        lhsT=s["onesf"][:, 0:64], rhs=s["rdlrow"],
                        start=True, stop=True, inc=("st", 1))
            self.W("vector", "st", tv)
            obufv[h] = self.E("vector", "tensor_mul",
                              out=s["obuf"][pb_:pb_ + 64, f * 512:(f + 1) * 512],
                              in0=a_["oraw"][0:64, :], in1=p["pc"][0:64, :],
                              inc=("sv", 1))

        for h in range(9):
            if h < 8:
                red[h] = {}
                A_mm(h, 0)
            for it in range(4):
                cc = C(h - 1, it) if h >= 1 else None
                if h < 8 and it < 3:
                    A_mm(h, it + 1)
                if h >= 1:
                    PV(h - 1, it, *cc)
            if h < 8:
                B(h)
            if h >= 1:
                E_(h - 1)
        self.W("tensor", "sv", obufv[7])

    def emit_wo(self, half, l, tc):
        s, p = self.s, self.p
        c0 = tc * WIN
        evac = {}
        for dt_ in range(NPT):
            pf = p["pa"] if dt_ % 2 == 0 else p["pb"]
            if dt_ >= 2:
                self.W("tensor", "sv", evac[dt_ - 2])
            for kk in range(4):
                self.E("tensor", "matmul", out=pf[:],
                       lhsT=s["wo"][:, kk * 1024 + dt_ * 128:kk * 1024 + (dt_ + 1) * 128],
                       rhs=s["obuf"][:, kk * 512:(kk + 1) * 512],
                       start=(kk == 0), stop=(kk == 3))
            tv = self.close("tensor")
            self.W("vector", "st", tv)
            evac[dt_] = self.E("vector", "tensor_add",
                               out=self.xcol(dt_, c0, WIN), in0=pf[:],
                               in1=self.xcol(dt_, c0, WIN), inc=("sv", 1))

    def emit_ffn(self, half, l):
        s, p, dr, a_ = self.s, self.p, self.dram, self.a
        self.bar()
        self.W("gpsimd", "swc", self.wconv_done)
        for tc in range(NTC):
            c0 = tc * WIN
            mdone = {}
            wld = {}
            tmm = {}
            for hh in range(NHH):
                buf = hh % 2
                psem = "sda" if hh % 2 == 0 else "sdb"
                if tc == 0 and hh < 2:
                    wld[hh] = self.ffn_pre[hh]
                else:
                    if hh >= 2:
                        self.W("gpsimd", "st", tmm[hh - 2])
                    else:
                        self.W("gpsimd", "st", self.ffn_prev1[buf])
                    wld[hh] = self.E("gpsimd", "dma_start",
                           out=s["w1c"][:, buf * 2048:(buf + 1) * 2048]
                               .rearrange("p (c n) -> p c n", c=8),
                           in_=dr[f"w1pb{l}"][:, hh * 256:(hh + 1) * 256]
                               .rearrange("(c p) n -> p c n", p=128),
                           inc=(psem, 16))
                pa_ = p["pa"] if hh % 2 == 0 else p["pc"]
                pg_ = p["pb"] if hh % 2 == 0 else p["pd"]
                self.W("tensor", psem, wld[hh])
                if hh >= 2:
                    self.W("tensor", "sv", mdone[hh - 2])
                for kk in range(NPT):
                    self.E("tensor", "matmul", out=pa_[:],
                           lhsT=s["w1c"][:, buf * 2048 + kk * 256:
                                         buf * 2048 + kk * 256 + 128],
                           rhs=self.xbcol(kk, c0, WIN),
                           start=(kk == 0), stop=(kk == NPT - 1))
                for kk in range(NPT):
                    self.E("tensor", "matmul", out=pg_[:],
                           lhsT=s["w1c"][:, buf * 2048 + kk * 256 + 128:
                                         buf * 2048 + kk * 256 + 256],
                           rhs=self.xbcol(kk, c0, WIN),
                           start=(kk == 0), stop=(kk == NPT - 1))
                tv = self.close("tensor")
                tmm[hh] = tv
                self.W("scalar", "st", tv)
                sv = self.E("scalar", "activation", out=a_["gelu"], in_=pg_[:],
                            func=AF.Gelu,
                            bias=s[f"b1p{l}"][:, 2 * hh + 1:2 * hh + 2],
                            inc=("ss", 1))
                self.W("vector", "ss", sv)
                mdone[hh] = self.E("vector", "scalar_tensor_tensor",
                       out=a_["msb"][:, hh * 512:(hh + 1) * 512],
                       in0=pa_[:], scalar=s[f"b1p{l}"][:, 2 * hh:2 * hh + 1],
                       in1=a_["gelu"], op0=ALU.add, op1=ALU.mult, inc=("sv", 1))
            self.ffn_prev1 = {0: tmm[NHH - 2], 1: tmm[NHH - 1]}
            self.W("tensor", "sv", self.cnt("sv"))
            banks = [p["pa"], p["pb"], p["pc"], p["pd"]]
            for dgrp in range(2):
                tmm2 = {}
                for hh in range(NHH):
                    buf = hh % 2
                    if hh >= 2:
                        self.W("gpsimd", "st", tmm2[hh - 2])
                    else:
                        self.W("gpsimd", "st", self.ffn_prev2[buf])
                    psem = "sdc" if hh % 2 == 0 else "sdd"
                    wl = self.E("gpsimd", "dma_start",
                           out=s["w2c"][:, buf * 1024:(buf + 1) * 1024],
                           in_=dr[f"w2pb{l}"][hh * 128:(hh + 1) * 128, :],
                           inc=(psem, 16))
                    self.W("tensor", psem, wl)
                    for di in range(4):
                        d = dgrp * 4 + di
                        self.E("tensor", "matmul", out=banks[di][:],
                               lhsT=s["w2c"][:, buf * 1024 + d * 128:
                                             buf * 1024 + (d + 1) * 128],
                               rhs=a_["msb"][:, hh * 512:(hh + 1) * 512],
                               start=(hh == 0), stop=(hh == NHH - 1))
                    tmm2[hh] = self.close("tensor")
                tv = self.close("tensor")
                self.W("vector", "st", tv)
                self.ffn_prev2 = {0: tmm2[NHH - 2], 1: tmm2[NHH - 1]}
                for di in range(4):
                    d = dgrp * 4 + di
                    self.E("vector", "scalar_tensor_tensor",
                           out=self.xcol(d, c0, WIN), in0=banks[di][:],
                           scalar=s[f"b2c{l}"][:, d:d + 1],
                           in1=self.xcol(d, c0, WIN), op0=ALU.add, op1=ALU.add,
                           inc=("sv", 1))
                self.W("tensor", "sv", self.cnt("sv"))

    def emit_final(self, half):
        s, p, dr, a_ = self.s, self.p, self.dram, self.a
        dmas = {}
        adds = {}
        PS = ("sfa", "sfb", "sfc", "sfd")
        for i in range(4):
            kk, c = i // 4, i % 4
            dmas[i] = self.E("sync", "dma_start",
                   out=a_["x1buf"][:, i * 512:(i + 1) * 512],
                   in_=dr["x1s"][kk * 128:(kk + 1) * 128,
                                 half * HALF + c * WIN:half * HALF + (c + 1) * WIN],
                   inc=(PS[i], 16))
        self.bar()
        for i in range(32):
            kk, c = i // 4, i % 4
            buf = i % 4
            if i >= 4:
                self.W("sync", "sv", adds[i - 4])
                dmas[i] = self.E("sync", "dma_start",
                       out=a_["x1buf"][:, buf * 512:(buf + 1) * 512],
                       in_=dr["x1s"][kk * 128:(kk + 1) * 128,
                                     half * HALF + c * WIN:half * HALF + (c + 1) * WIN],
                       inc=(PS[buf], 16))
            self.W("vector", PS[buf], dmas[i])
            adds[i] = self.E("vector", "tensor_add",
                   out=self.xcol(kk, c * WIN, WIN),
                   in0=self.xcol(kk, c * WIN, WIN),
                   in1=a_["x1buf"][:, buf * 512:(buf + 1) * 512], inc=("sv", 1))
        self.emit_ln(l=None, shifted=False, write_xb=False)
        for kk in range(NPT):
            self.E("sync", "dma_start",
                   out=dr["out"][kk * 128:(kk + 1) * 128,
                                 half * HALF:(half + 1) * HALF],
                   in_=self.xcol(kk, 0, HALF), inc=("sst", 16))

    def emit_taps(self):
        pass

    def mtap(self, name, ap, shape, dt=F32):
        if name not in self.taps:
            return
        t = self.tap(name, shape, dt)
        self.bar()
        self.E("sync", "dma_start", out=t, in_=ap, inc=("sst", 16))
        self.bar()

    def finish(self):
        self.bar()


def build_kernel(taps=(), stop_after=None):
    k = Kern(taps=taps, stop_after=stop_after)
    nc = k.build()
    return k, nc


def host_constants():
    c = {}
    c["identb"] = np.eye(128, dtype=ml_dtypes.bfloat16)
    c["onesf"] = np.ones((1, 128), np.float32)
    c["onesb"] = np.ones((1, 128), ml_dtypes.bfloat16)
    c["onesbb"] = np.ones((128, 128), ml_dtypes.bfloat16)
    c["onescb"] = np.ones((128, 1), ml_dtypes.bfloat16)
    c["epscol"] = np.full((128, 1), 1e-5, np.float32)
    c["eps6col"] = np.full((128, 1), 1e-6, np.float32)
    c["pihalf"] = np.full((128, 1), PI / 2, np.float32)
    jj, ii = np.meshgrid(np.arange(128), np.arange(128), indexing="ij")
    c["maskl"] = np.where(jj < ii, -1e30, 0.0).astype(ml_dtypes.bfloat16)
    c["maskr"] = np.where(jj > ii, -1e30, 0.0).astype(ml_dtypes.bfloat16)
    P = np.zeros((128, 128), np.float32)
    for b in range(2):
        for e in range(64):
            P[b * 64 + (e + 32) % 64, b * 64 + e] = 1.0
    c["rotp"] = P.astype(ml_dtypes.bfloat16)
    S = np.zeros((16, 8 * 128), np.float32)
    for a in range(8):
        for p_ in range(128):
            S[2 * a + p_ // 64, a * 128 + p_] = 1.0
    c["selck"] = S.astype(ml_dtypes.bfloat16)
    S = np.zeros((128, 8 * 16), np.float32)
    for f in range(8):
        for k_ in range(128):
            r = 2 * f + k_ // 64
            S[k_, f * 16 + r] = 1.0
    c["segsel"] = S.astype(ml_dtypes.bfloat16)
    Dh, W_ = 64, 512
    half = np.arange(0, Dh, 2, dtype=np.float64)
    inv = 1.0 / 10000.0 ** (half / Dh)
    t = np.arange(T, dtype=np.float64)
    freqs = np.concatenate([t[:, None] * inv[None, :]] * 2, -1)
    base = (half + 0.4 * Dh) / (1.4 * Dh)
    power = (t - T // 2) / (W_ // 2)
    scale = np.concatenate([base[None, :] ** power[:, None]] * 2, -1)
    cosf, sinf = np.cos(freqs), np.sin(freqs)
    sgn = np.concatenate([-np.ones(32), np.ones(32)])
    dup = lambda M: np.concatenate([M, M], axis=0).astype(ml_dtypes.bfloat16)
    c["tcs"] = dup((8.0 * cosf * scale).T)
    c["tss"] = dup((8.0 * sinf * scale * sgn[None, :]).T)
    c["tci"] = dup((cosf / scale).T)
    c["tsi"] = dup((sinf / scale * sgn[None, :]).T)
    return c


def host_core_inputs(ins, core):
    d = {}
    q = np.asarray(ins["L0_attn_query"][core], np.float32)
    d["x0t"] = np.ascontiguousarray(q.T)
    WQ = np.asarray(ins["L0_attn_W_Q"], np.float32)[:, :, 0].T
    WK = np.asarray(ins["L0_attn_W_K"], np.float32)[:, :, 0].T
    wqk = np.zeros((D, 64), np.float32)
    wqk[:, 0:16] = WQ
    wqk[:, 32:48] = WK
    d["wqk"] = wqk
    WV = np.asarray(ins["L0_attn_W_V"], np.float32)
    d["wv"] = np.ascontiguousarray(WV.transpose(1, 0, 2).reshape(D, 1024))
    WO = np.asarray(ins["L0_attn_W_O"], np.float32)
    d["wop"] = np.ascontiguousarray(WO.transpose(2, 1, 0).reshape(1024, D))
    for l in range(2):
        d[f"wqkv{l}"] = np.ascontiguousarray(np.asarray(ins["L0_ffn_wqkv"][l], np.float32))
        d[f"wo{l}"] = np.ascontiguousarray(np.asarray(ins["L0_ffn_wo"][l], np.float32))
        w1 = np.asarray(ins["L0_ffn_w1"][l], np.float32)
        a = np.zeros((D, HID), np.float32); a[:, :2730] = w1[:, :2730]
        g = np.zeros((D, HID), np.float32); g[:, :2730] = w1[:, 2730:]
        w1p = np.zeros((D, NHH, 2, 128), np.float32)
        for hh in range(NHH):
            w1p[:, hh, 0] = a[:, hh * 128:(hh + 1) * 128]
            w1p[:, hh, 1] = g[:, hh * 128:(hh + 1) * 128]
        d[f"w1p{l}"] = np.ascontiguousarray(w1p.reshape(D, 2 * HID))
        w2 = np.asarray(ins["L0_ffn_w2"][l], np.float32)
        w2p = np.zeros((HID, D), np.float32); w2p[:2730] = w2
        d[f"w2p{l}"] = w2p
        b1 = np.asarray(ins["L0_ffn_b1"][l], np.float32)
        b1a = np.zeros(HID, np.float32); b1a[:2730] = b1[:2730]
        b1g = np.zeros(HID, np.float32); b1g[:2730] = b1[2730:]
        b1p = np.zeros((128, 2 * NHH), np.float32)
        for hh in range(NHH):
            b1p[:, 2 * hh] = b1a[hh * 128:(hh + 1) * 128]
            b1p[:, 2 * hh + 1] = b1g[hh * 128:(hh + 1) * 128]
        d[f"b1p{l}"] = b1p
        d[f"b2c{l}"] = np.ascontiguousarray(
            np.asarray(ins["L0_ffn_b2"][l], np.float32).reshape(NPT, 128).T)
        d[f"lng{l}"] = np.ascontiguousarray(
            np.asarray(ins["L0_ffn_ln_g"][l], np.float32).reshape(NPT, 128).T)
        d[f"lnb{l}"] = np.ascontiguousarray(
            np.asarray(ins["L0_ffn_ln_b"][l], np.float32).reshape(NPT, 128).T)
        qs = np.asarray(ins["L0_ffn_q_scale"][l], np.float32)
        ks = np.asarray(ins["L0_ffn_k_scale"][l], np.float32)
        d[f"qs{l}"] = np.ascontiguousarray(np.concatenate([qs, qs]).reshape(128, 1))
        d[f"ks{l}"] = np.ascontiguousarray(np.concatenate([ks, ks]).reshape(128, 1))
    return d


from concourse.bass_utils import run_bass_kernel_spmd


def kernel(**inputs):
    consts = host_constants()
    k = Kern(taps=(), stop_after=None)
    nc = k.build()
    valid = set(k.dram.keys())
    in_maps = []
    for core in range(8):
        cin = host_core_inputs(inputs, core)
        m = {n: np.asarray(v) for n, v in cin.items() if n in valid}
        m.update({n: np.asarray(v) for n, v in consts.items() if n in valid})
        in_maps.append(m)
    res = run_bass_kernel_spmd(nc, in_maps, core_ids=list(range(8)))
    out = np.stack([np.ascontiguousarray(r["out"].T) for r in res.results], axis=0)
    return out.astype(np.float32)
